# version 20
# baseline (speedup 1.0000x reference)
import contextlib
from collections import defaultdict

import numpy as np
import ml_dtypes

import concourse.bass as bass
import concourse.mybir as mybir
from concourse.bass_utils import run_bass_kernel_spmd

F32 = mybir.dt.float32
BF16 = mybir.dt.bfloat16
I32 = mybir.dt.int32
AF = mybir.ActivationFunctionType
ALU = mybir.AluOpType
AX = mybir.AxisListType
BFD = ml_dtypes.bfloat16

NCORE = 8
SEQ = 2048
NSEQ = 4
D = 1024
HDIM = 4096
NH = 16
TOK = NSEQ * SEQ
SUP = 512
NSUP = TOK // SUP
EPS = 1e-5
RS = float(2 ** 24)
RSI = float(2 ** -24)

WNAMES = [
    "L0_attn_Wq", "L0_attn_bq", "L0_attn_Wk", "L0_attn_bk", "L0_attn_Wv",
    "L0_attn_bv", "L0_attn_Wo", "L0_attn_bo", "L0_ffn_W_exp", "L0_ffn_b_exp",
    "L0_ffn_up_weights", "L0_ffn_up_bias", "L0_ffn_down_weights",
    "L0_ffn_W_con", "L0_ffn_b_con", "L0_ffn_down_bias",
]


class Prog:
    ENGINES = ("T", "V", "S", "G", "Y")

    def __init__(self, nc, stack):
        self.nc = nc
        self.stack = stack
        self.ops = []
        self.tag_w = {}
        self.tag_r = defaultdict(list)
        self.clock = {e: {} for e in self.ENGINES}
        self.op_clock = []
        self.dma_count = defaultdict(int)
        self.dma_clock = {}
        self.sems = {}

    def _sem(self, name):
        if name not in self.sems:
            self.sems[name] = self.stack.enter_context(self.nc.semaphore(name))
        return self.sems[name]

    @staticmethod
    def _join(dst, src):
        for k, v in src.items():
            if dst.get(k, -1) < v:
                dst[k] = v

    def _covered(self, clk, m):
        if m[0] == "e":
            return clk.get(("e", m[1]), -1) >= m[2]
        return clk.get(("d", m[1]), -1) >= m[2]

    def op(self, engine, builders, reads=(), writes=(), dma=None, accum_tags=()):
        if callable(builders):
            builders = [builders]
        idx = len(self.ops)
        e = engine
        waits = []

        def need(m):
            if m is None:
                return
            if m[0] == "d":
                m = ("d", m[1], self.dma_count[m[1]] * 16)
            if not self._covered(self.clock[e], m):
                if m not in waits:
                    waits.append(m)

        reads = list(reads)
        writes = list(writes)
        for t in reads:
            w = self.tag_w.get(t)
            if w is not None:
                if w[0] == "e" and w[1] == e and t in accum_tags:
                    pass
                else:
                    need(w)
        for t in writes:
            w = self.tag_w.get(t)
            if w is not None and not (w[0] == "e" and w[1] == e and t in accum_tags):
                need(w)
            for r in self.tag_r[t]:
                if r[0] == "e" and r[1] == e:
                    continue
                need(r)
        for t in reads:
            if t.startswith("P:") and t not in writes:
                for r in self.tag_r[t]:
                    if r[0] == "e" and r[1] == e:
                        continue
                    need(r)

        if dma is not None:
            self.dma_count[dma] += 1
            marker = ("d", dma, self.dma_count[dma] * 16)
        else:
            marker = ("e", e, idx)

        myclk = self.clock[e]
        for m in waits:
            if m[0] == "e":
                self._join(myclk, self.op_clock[m[2]])
                myclk[("e", m[1])] = max(myclk.get(("e", m[1]), -1), m[2])
            else:
                self._join(myclk, self.dma_clock[(m[1], m[2])])
                myclk[("d", m[1])] = max(myclk.get(("d", m[1]), -1), m[2])

        snap = dict(myclk)
        if dma is not None:
            snap[("d", dma)] = self.dma_count[dma] * 16
            self.dma_clock[(dma, self.dma_count[dma] * 16)] = snap
        else:
            snap[("e", e)] = idx
        self.op_clock.append(snap)

        for t in writes:
            self.tag_w[t] = marker
            self.tag_r[t] = []
        for t in reads:
            if t not in writes:
                self.tag_r[t].append(marker)

        self.ops.append(dict(engine=e, builders=builders, waits=waits, dma=dma,
                             idx=idx, inc_needed=False))
        return idx

    def fence(self):
        markers = []
        last_eng = {}
        for o in self.ops:
            if o["dma"] is None and o["builders"]:
                last_eng[o["engine"]] = o["idx"]
        for e, i in last_eng.items():
            markers.append(("e", e, i))
        for dsem, cnt in self.dma_count.items():
            markers.append(("d", dsem, cnt * 16))
        for e in self.ENGINES:
            idx = len(self.ops)
            waits = [m for m in markers
                     if not (m[0] == "e" and m[1] == e)
                     and not self._covered(self.clock[e], m)]
            myclk = self.clock[e]
            for m in waits:
                if m[0] == "e":
                    self._join(myclk, self.op_clock[m[2]])
                    myclk[("e", m[1])] = max(myclk.get(("e", m[1]), -1), m[2])
                else:
                    self._join(myclk, self.dma_clock[(m[1], m[2])])
                    myclk[("d", m[1])] = max(myclk.get(("d", m[1]), -1), m[2])
            snap = dict(myclk)
            snap[("e", e)] = idx
            self.op_clock.append(snap)
            self.ops.append(dict(engine=e, builders=[], waits=waits, dma=None,
                                 idx=idx, inc_needed=False))

    def emit(self):
        for o in self.ops:
            for m in o["waits"]:
                if m[0] == "e":
                    self.ops[m[2]]["inc_needed"] = True
        eng_rank = {}
        counters = defaultdict(int)
        for o in self.ops:
            if o["dma"] is None and o["inc_needed"]:
                counters[o["engine"]] += 1
                eng_rank[o["idx"]] = counters[o["engine"]]
        for e in self.ENGINES:
            if counters[e]:
                self._sem(f"sem_{e}")
        for dsem in list(self.dma_count):
            self._sem(dsem)

        per_engine = {e: [] for e in self.ENGINES}
        for o in self.ops:
            per_engine[o["engine"]].append(o)
        block = self.stack.enter_context(self.nc.Block())

        def run_stream(eng_obj, ops):
            for o in ops:
                for m in o["waits"]:
                    if m[0] == "e":
                        eng_obj.wait_ge(self._sem(f"sem_{m[1]}"), eng_rank[m[2]])
                    else:
                        eng_obj.wait_ge(self._sem(m[1]), m[2])
                n = len(o["builders"])
                for bi, b in enumerate(o["builders"]):
                    ins = b(eng_obj)
                    if bi == n - 1:
                        if o["dma"] is not None:
                            ins.then_inc(self._sem(o["dma"]), 16)
                        elif o["inc_needed"]:
                            ins.then_inc(self._sem(f"sem_{o['engine']}"), 1)

        section = {"T": block.tensor, "V": block.vector, "S": block.scalar,
                   "G": block.gpsimd, "Y": block.sync}
        for e in self.ENGINES:
            if per_engine[e]:
                def make(ops=per_engine[e]):
                    return lambda eng_obj: run_stream(eng_obj, ops)
                section[e](make())
        return block


class Arena:
    def __init__(self, tensor, name, cols, blk=2048):
        self.t = tensor
        self.name = name
        self.cols = cols
        self.blk = blk
        self.m = {}

    def place(self, name, off, n):
        assert off + n <= self.cols, f"{self.name}:{name} overflow {off}+{n}>{self.cols}"
        self.m[name] = (off, n)

    def view(self, name, p0=0, pn=128, c0=0, cn=None):
        off, n = self.m[name]
        if cn is None:
            cn = n - c0
        assert c0 + cn <= n, f"view {name} {c0}+{cn}>{n}"
        return self.t[p0:p0 + pn, off + c0: off + c0 + cn]

    def tags(self, name, c0=0, cn=None):
        off, n = self.m[name]
        if cn is None:
            cn = n - c0
        lo = (off + c0) // self.blk
        hi = (off + c0 + cn - 1) // self.blk
        return [f"{self.name}:{i}" for i in range(lo, hi + 1)]


def build_nc():
    nc = bass.Bass("TRN2", target_bir_lowering=False)

    x_in = nc.dram_tensor("x", [TOK, D], F32, kind="ExternalInput").ap()
    y_out = nc.dram_tensor("y", [TOK, D], F32, kind="ExternalOutput").ap()
    w_exp_in = nc.dram_tensor("L0_ffn_W_exp", [D, HDIM], F32, kind="ExternalInput").ap()
    b_exp_in = nc.dram_tensor("L0_ffn_b_exp", [HDIM], F32, kind="ExternalInput").ap()
    up_w_in = nc.dram_tensor("L0_ffn_up_weights", [12, 2048, 2, 2], F32, kind="ExternalInput").ap()
    up_b_in = nc.dram_tensor("L0_ffn_up_bias", [HDIM], F32, kind="ExternalInput").ap()
    dn_w_in = nc.dram_tensor("L0_ffn_down_weights", [12, 2048, 2, 2], F32, kind="ExternalInput").ap()
    w_con_in = nc.dram_tensor("L0_ffn_W_con", [HDIM, D], F32, kind="ExternalInput").ap()
    b_con_in = nc.dram_tensor("L0_ffn_b_con", [D], F32, kind="ExternalInput").ap()
    dn_b_in = nc.dram_tensor("L0_ffn_down_bias", [D], F32, kind="ExternalInput").ap()
    wq_in = nc.dram_tensor("L0_attn_Wq", [NH, D, 64], F32, kind="ExternalInput").ap()
    bq_in = nc.dram_tensor("L0_attn_bq", [NH, 64], F32, kind="ExternalInput").ap()
    wk_in = nc.dram_tensor("L0_attn_Wk", [NH, D, 64], F32, kind="ExternalInput").ap()
    bk_in = nc.dram_tensor("L0_attn_bk", [NH, 64], F32, kind="ExternalInput").ap()
    wv_in = nc.dram_tensor("L0_attn_Wv", [NH, D, 64], F32, kind="ExternalInput").ap()
    bv_in = nc.dram_tensor("L0_attn_bv", [NH, 64], F32, kind="ExternalInput").ap()
    wo_in = nc.dram_tensor("L0_attn_Wo", [D, D], F32, kind="ExternalInput").ap()
    bo_in = nc.dram_tensor("L0_attn_bo", [D], F32, kind="ExternalInput").ap()
    ident_in = nc.dram_tensor("cident", [128, 128], BF16, kind="ExternalInput").ap()
    tident_in = nc.dram_tensor("ctident", [128, HDIM], BF16, kind="ExternalInput").ap()
    bigmask_in = nc.dram_tensor("cbigmask", [128, 1024], BF16, kind="ExternalInput").ap()
    mask01_in = nc.dram_tensor("cmask01", [128, 1024], BF16, kind="ExternalInput").ap()
    headones_in = nc.dram_tensor("cheadones", [128, 128], BF16, kind="ExternalInput").ap()

    x1_d = nc.dram_tensor("x1", [TOK, D], F32, kind="Internal").ap()
    x2_d = nc.dram_tensor("x2", [TOK, D], F32, kind="Internal").ap()
    wexp_s = nc.dram_tensor("wexp_s", [32, 128, 8, 128], BF16, kind="Internal").ap()
    wcon_s = nc.dram_tensor("wcon_s", [32, 128, 1024], BF16, kind="Internal").ap()
    wqd = nc.dram_tensor("wqd", [8, 128, 8, 128], BF16, kind="Internal").ap()
    wkd = nc.dram_tensor("wkd", [8, 128, 8, 128], BF16, kind="Internal").ap()
    up_bf = nc.dram_tensor("up_bf", [12, 2048, 2, 2], BF16, kind="Internal").ap()
    dn_bf = nc.dram_tensor("dn_bf", [12, 2048, 2, 2], BF16, kind="Internal").ap()

    st = contextlib.ExitStack()
    with st:
        kident = st.enter_context(nc.sbuf_tensor("kident", [128, 128], BF16))
        khead = st.enter_context(nc.sbuf_tensor("khead", [128, 128], BF16))
        kmask = st.enter_context(nc.sbuf_tensor("kmask", [128, 1024], BF16))
        kmask01 = st.enter_context(nc.sbuf_tensor("kmask01", [128, 1024], BF16))
        kones = st.enter_context(nc.sbuf_tensor("kones", [128, 512], BF16))
        krsi = st.enter_context(nc.sbuf_tensor("krsi", [1, 64], BF16))
        bias1 = st.enter_context(nc.sbuf_tensor("bias1", [128, 32], F32))
        negc = st.enter_context(nc.sbuf_tensor("negc", [128, 16], F32))
        stats = st.enter_context(nc.sbuf_tensor("stats", [128, 8], F32))
        nsc = [st.enter_context(nc.sbuf_tensor(f"nsc{i}", [128, 12], F32)) for i in range(4)]
        nsci = st.enter_context(nc.sbuf_tensor("nsci", [128, 8], I32))
        keps = st.enter_context(nc.sbuf_tensor("keps", [128, 1], F32))
        qmax = st.enter_context(nc.sbuf_tensor("qmax", [128, 8], F32))
        kqmax = st.enter_context(nc.sbuf_tensor("kqmax", [128, 8], F32))
        ctmp = [st.enter_context(nc.sbuf_tensor(f"ctmp{i}", [128, 8], F32)) for i in range(2)]
        negc8 = st.enter_context(nc.sbuf_tensor("negc8", [128, 8], BF16))

        ABF_COLS = 73792
        AF_COLS = 13568
        abf_t = st.enter_context(nc.sbuf_tensor("abf", [128, ABF_COLS], BF16))
        af_t = st.enter_context(nc.sbuf_tensor("af", [128, AF_COLS], F32))
        A = Arena(abf_t, "bf", ABF_COLS, blk=512)
        Z = Arena(af_t, "f3", AF_COLS, blk=1024)

        ps01 = st.enter_context(nc.psum_tensor("ps01", [128, 1024], F32))
        ps45 = st.enter_context(nc.psum_tensor("ps45", [128, 1024], F32))
        PS = [ps01[:, 0:512], ps01[:, 512:1024]] + [
            st.enter_context(nc.psum_tensor(f"ps{i}", [128, 512], F32)) for i in range(2, 4)] + [
            ps45[:, 0:512], ps45[:, 512:1024]]
        PT = st.enter_context(nc.psum_tensor("pt", [128, 1024], BF16))
        PT2 = st.enter_context(nc.psum_tensor("pt2", [128, 1024], BF16))
        PTS = [(PT, "P:T"), (PT2, "P:T2")]

        p = Prog(nc, st)

        p.op("Y", lambda y: y.dma_start(out=kident[:, :], in_=ident_in), writes=["kident"], dma="dc")
        p.op("Y", lambda y: y.dma_start(out=khead[:, :], in_=headones_in), writes=["khead"], dma="dc")
        p.op("Y", lambda y: y.dma_start(out=kmask[:, :], in_=bigmask_in), writes=["kmask"], dma="dc")
        p.op("Y", lambda y: y.dma_start(out=kmask01[:, :], in_=mask01_in), writes=["kmask01"], dma="dc")
        p.op("V", lambda v: v.memset(kones[:, :], 1.0), writes=["kones"])
        p.op("V", lambda v: v.memset(krsi[:, :], 1.0), writes=["krsi"])
        p.op("V", lambda v: v.memset(keps[:, :], EPS), writes=["keps"])

        A.place("R2", 0, 32768)
        A.place("rowp", 32768, 8192)
        A.place("wtb", 73728, 64)
        A.place("tbuf", 40960, 4096)
        A.place("coefrep", 45056, 4096)
        A.place("rawk", 49152, 8192)
        A.place("ttmp0", 57344, 2048)
        A.place("ttmp1", 59392, 2048)
        A.place("stag", 61440, 8192)
        A.place("wes", 40960, 8192)
        A.place("xn", 49152, 4096)
        A.place("xnT", 53248, 4096)
        A.place("xnT2", 36864, 4096)
        A.place("hh", 57344, 16384)
        A.place("wv", 0, 8192)
        A.place("wos", 8192, 8192)
        A.place("wqk", 16384, 4096)
        A.place("axn", 20480, 2048)
        A.place("axnT", 22528, 4096)
        A.place("qs", 26624, 4096)
        A.place("pb2", 30720, 2048)
        A.place("nsq", 30720, 2048)
        A.place("os", 32768, 4096)
        A.place("ks", 36864, 16384)
        A.place("vs", 53248, 16640)
        A.place("axn2", 70144, 2048)
        A.place("pb3", 72192, 1024)

        Z.place("xb", 0, 8192)
        Z.place("yb", 8192, 2048)
        Z.place("fscr", 0, 4096)
        Z.place("fident", 10240, 128)
        Z.place("hcoefU", 10368, 320)
        Z.place("hcoefD", 10688, 320)
        Z.place("xa", 0, 8192)
        Z.place("x2b", 8192, 2048)
        Z.place("srow", 10240, 512)
        Z.place("srep", 10752, 512)
        Z.place("kbrowF", 11264, 2048)
        Z.place("srowRF", 13312, 256)

        rowp = lambda pp, c0, cn: A.view("rowp", pp, 1, c0, cn)
        kbrow = lambda c0, cn: Z.view("kbrowF", 0, 1, 0, 2048).bitcast(BF16)[0:1, c0:c0 + cn]
        kbt = Z.tags("kbrowF")
        PBS = [("pb2", 0), ("pb2", 1024), ("pb3", 0)]
        srow_v = lambda: Z.view("srow", 0, 1, 0, 512)
        srowR_v = lambda: Z.view("srowRF", 0, 1, 0, 256).bitcast(BF16)
        srep_v = lambda: Z.view("srep", 0, 64, 0, 512)

        def load_hcoef(w_dram, name):
            for si, s in enumerate(range(7, 12)):
                p.op("S", lambda y, s=s, si=si, w_dram=w_dram, name=name: y.dma_start(
                    out=Z.view(name, 0, 128, si * 64, 64).rearrange("p (q ab) -> p q ab", ab=4),
                    in_=w_dram[s].rearrange("(q p) a b -> p q (a b)", p=128)),
                    writes=Z.tags(name, si * 64, 64), dma="dhc")

        load_hcoef(up_w_in, "hcoefU")
        load_hcoef(dn_w_in, "hcoefD")

        for src, dst, nm in ((up_w_in, up_bf, "ubf"), (dn_w_in, dn_bf, "dbf")):
            p.op("G", lambda g, src=src, dst=dst: g.dma_start(
                out=dst.rearrange("s k a b -> (s k a b)").unsqueeze(0),
                in_=src.rearrange("s k a b -> (s k a b)").unsqueeze(0)),
                writes=[f"dram_{nm}"], dma="dpc")

        def pack_qk():
            for pr in range(8):
                for hh in range(2):
                    for src, dst, nm in ((wq_in, wqd, "wq"), (wk_in, wkd, "wk")):
                        p.op("G", lambda g, src=src, dst=dst, pr=pr, hh=hh: g.dma_start(
                            out=dst[pr][:, :, 64 * hh:64 * hh + 64],
                            in_=src[2 * pr + hh].rearrange("(c p) d -> p c d", p=128)),
                            writes=[f"dram_{nm}:{pr}:{hh}"], dma=f"dpk{pr % 2}")

        def build_T(w_dram, wtag, transposed, reverse):
            p.op("Y", lambda y: y.dma_start(out=A.view("tbuf"), in_=tident_in),
                 writes=A.tags("tbuf"), dma="dc")
            order = list(range(7))
            if reverse:
                order = order[::-1]
            i01, i10 = (2, 1) if transposed else (1, 2)
            for st_i in order:
                s = 1 << st_i
                snm = "rowp" if st_i % 2 == 0 else "stag"
                for lh in range(2):
                    p.op("Y", lambda y, st_i=st_i, lh=lh, snm=snm, w_dram=w_dram: y.dma_start(
                        out=A.view(snm, 0, 1, lh * 4096, 4096),
                        in_=w_dram[st_i].rearrange("k a b -> (k a b)").unsqueeze(0)[0:1, lh * 4096:(lh + 1) * 4096]),
                        reads=[wtag],
                        writes=A.tags(snm, lh * 4096, 4096), dma="dcf")
                for half in range(2):
                    for b in range(8):
                        pv, ptag = PS[b % 6], f"P:{b % 6}"
                        p.op("T", lambda t, half=half, b=b, snm=snm, pv=pv: t.matmul(
                            pv[:, :], kones[0:1, 0:128],
                            A.view(snm, 0, 1, half * 4096 + b * 512, 512), start=True, stop=True),
                            reads=A.tags(snm, half * 4096 + b * 512, 512) + ["kones"], writes=[ptag])
                        p.op("S", lambda s_, b=b, pv=pv: s_.copy(
                            A.view("coefrep", 0, 128, b * 512, 512), pv[:, :]),
                            reads=[ptag], writes=A.tags("coefrep", b * 512, 512))
                    f0 = half * 2048
                    tags_t = A.tags("tbuf", f0, 2048)
                    tags_c = A.tags("coefrep")

                    def A_v(st_i=st_i, s=s, f0=f0):
                        tv = A.view("tbuf", 0, 128, f0, 2048)
                        tv4 = tv.rearrange("p (nb two s) -> p nb two s", two=2, s=s)
                        return tv4[:, :, 0, :]

                    def B_v(st_i=st_i, s=s, f0=f0):
                        tv = A.view("tbuf", 0, 128, f0, 2048)
                        tv4 = tv.rearrange("p (nb two s) -> p nb two s", two=2, s=s)
                        return tv4[:, :, 1, :]

                    def W_v(j, s=s):
                        cr = A.view("coefrep")
                        cr4 = cr.rearrange("p (nb s four) -> p nb s four", s=s, four=4)
                        return cr4[:, :, :, j]

                    def T_v(name, s=s):
                        return A.view(name, 0, 128, 0, 1024).rearrange("p (nb s) -> p nb s", s=s)

                    p.op("V", lambda v, B_v=B_v, W_v=W_v, T_v=T_v, i01=i01: v.tensor_tensor(
                        T_v("ttmp0"), B_v(), W_v(i01), ALU.mult),
                        reads=tags_t + tags_c, writes=A.tags("ttmp0", 0, 1024))
                    p.op("V", lambda v, B_v=B_v, W_v=W_v, T_v=T_v: v.tensor_tensor(
                        T_v("ttmp1"), B_v(), W_v(3), ALU.mult),
                        reads=tags_t + tags_c, writes=A.tags("ttmp1", 0, 1024))
                    p.op("V", lambda v, A_v=A_v, B_v=B_v, W_v=W_v, i10=i10: v.tensor_tensor(
                        B_v(), A_v(), W_v(i10), ALU.mult),
                        reads=tags_t + tags_c, writes=tags_t)
                    p.op("V", lambda v, B_v=B_v, T_v=T_v: v.tensor_tensor(
                        B_v(), B_v(), T_v("ttmp1"), ALU.add),
                        reads=tags_t + A.tags("ttmp1", 0, 1024), writes=tags_t)
                    p.op("V", lambda v, A_v=A_v, W_v=W_v: v.tensor_tensor(
                        A_v(), A_v(), W_v(0), ALU.mult),
                        reads=tags_t + tags_c, writes=tags_t)
                    p.op("V", lambda v, A_v=A_v, T_v=T_v: v.tensor_tensor(
                        A_v(), A_v(), T_v("ttmp0"), ALU.add),
                        reads=tags_t + A.tags("ttmp0", 0, 1024), writes=tags_t)

        def high_stages(tags_fn, view_fn, coef, transposed, reverse, width):
            order = list(range(5))
            if reverse:
                order = order[::-1]
            i01, i10 = (2, 1) if transposed else (1, 2)
            for si in order:
                stride = 1 << (7 + si)
                cstep = stride // 128
                pi = 0
                for c in range(32):
                    if c & cstep:
                        continue
                    c2 = c + cstep
                    col = (si * 16 + pi) * 4
                    pi += 1
                    w = lambda j, col=col, coef=coef: Z.view(coef, 0, 128, col + j, 1)
                    zt = Z.tags(coef, col, 4)
                    ta, tb_ = tags_fn(c), tags_fn(c2)
                    eng = "V"
                    tt0 = A.tags("ttmp0", 0, width) if width > 1 else A.tags("ttmp0", 0, 1)
                    tt1 = A.tags("ttmp1", 0, width) if width > 1 else A.tags("ttmp1", 0, 1)
                    p.op(eng, lambda v, c2=c2, w=w, i01=i01, width=width: v.tensor_scalar(
                        A.view("ttmp0", 0, 128, 0, width), view_fn(c2), w(i01), None, ALU.mult),
                        reads=tb_ + zt, writes=tt0)
                    p.op(eng, lambda v, c2=c2, w=w, width=width: v.tensor_scalar(
                        A.view("ttmp1", 0, 128, 0, width), view_fn(c2), w(3), None, ALU.mult),
                        reads=tb_ + zt, writes=tt1)
                    p.op(eng, lambda v, c=c, c2=c2, w=w, i10=i10, width=width: v.scalar_tensor_tensor(
                        view_fn(c2), view_fn(c), w(i10),
                        A.view("ttmp1", 0, 128, 0, width), ALU.mult, ALU.add),
                        reads=ta + tt1 + zt, writes=tb_)
                    p.op(eng, lambda v, c=c, w=w, width=width: v.scalar_tensor_tensor(
                        view_fn(c), view_fn(c), w(0),
                        A.view("ttmp0", 0, 128, 0, width), ALU.mult, ALU.add),
                        reads=ta + tt0 + zt, writes=ta)

        build_T(up_bf, "dram_ubf", transposed=False, reverse=False)

        p.op("G", lambda g: g.dma_start(out=rowp(32, 0, 4096), in_=b_exp_in.unsqueeze(0)),
             writes=A.tags("rowp"), dma="dcf")
        for cg in range(4):
            p.op("T", [
                (lambda t, c=8 * cg + j: t.transpose(
                    PT[:, 2 * c:2 * c + 1], rowp(32, 128 * c, 128), kident[32:33, 32:33]))
                for j in range(8)],
                reads=A.tags("rowp") + ["kident"], writes=["P:T"])
        p.op("V", lambda v: v.tensor_copy(
            A.view("wtb", 0, 128, 0, 32),
            PT[:, 0:64].rearrange("p (a b) -> p a b", b=2)[:, :, 0]),
            reads=["P:T"], writes=A.tags("wtb"))

        for k in range(8):
            kb = k % 2
            p.op("G", lambda g, k=k, kb=kb: g.dma_start(
                out=A.view("rawk", 0, 128, kb * 4096, 4096),
                in_=w_exp_in.rearrange("(c p) f -> c p f", p=128)[k]),
                writes=A.tags("rawk", kb * 4096, 4096), dma=f"dre{kb}")
            for cg in range(4):
                ptile, pttag = PTS[cg % 2]
                p.op("T", [
                    (lambda t, kb=kb, c=8 * cg + j, j=j, ptile=ptile: t.transpose(
                        ptile[:, 128 * j:128 * (j + 1)],
                        A.view("rawk", 0, 128, kb * 4096 + 128 * c, 128),
                        kident[:, :]))
                    for j in range(8)],
                    reads=A.tags("rawk", kb * 4096, 4096) + ["kident"], writes=[pttag])
                p.op("V", lambda v, k=k, cg=cg, ptile=ptile: v.tensor_copy(
                    A.view("R2").rearrange("p (c q) -> p c q", q=1024)[:, 8 * cg:8 * cg + 8, 128 * k:128 * (k + 1)],
                    ptile[:, :].rearrange("p (j q) -> p j q", q=128)),
                    reads=[pttag],
                    writes=[t for j in range(8) for t in A.tags("R2", (8 * cg + j) * 1024 + 128 * k, 128)])

        for c in range(32):
            for half in range(2):
                pv = PS[2 + (c * 2 + half) % 4]
                ptag = f"P:{2 + (c * 2 + half) % 4}"
                p.op("T", lambda t, c=c, half=half, pv=pv: t.matmul(
                    pv[:, :], A.view("tbuf", 0, 128, 128 * c, 128),
                    A.view("R2", 0, 128, c * 1024 + half * 512, 512),
                    start=True, stop=True),
                    reads=A.tags("tbuf", 128 * c, 128) + A.tags("R2", c * 1024 + half * 512, 512),
                    writes=[ptag])
                p.op("S", lambda s_, c=c, half=half, pv=pv: s_.copy(
                    A.view("R2", 0, 128, c * 1024 + half * 512, 512), pv[:, :]),
                    reads=[ptag], writes=A.tags("R2", c * 1024 + half * 512, 512))
        p.op("T", [
            (lambda t, c=c: t.matmul(PS[4][:, c:c + 1],
                                     A.view("tbuf", 0, 128, 128 * c, 128),
                                     A.view("wtb", 0, 128, c, 1), start=True, stop=True))
            for c in range(32)],
            reads=A.tags("tbuf") + A.tags("wtb"), writes=["P:4"])
        p.op("V", lambda v: v.tensor_copy(A.view("wtb", 0, 128, 0, 32), PS[4][:, 0:32]),
             reads=["P:4"], writes=A.tags("wtb"))

        high_stages(lambda c: A.tags("R2", c * 1024, 1024),
                    lambda c: A.view("R2", 0, 128, c * 1024, 1024),
                    "hcoefU", transposed=False, reverse=False, width=1024)
        high_stages(lambda c: A.tags("wtb"),
                    lambda c: A.view("wtb", 0, 128, c, 1),
                    "hcoefU", transposed=False, reverse=False, width=1)

        for k in range(8):
            kb = k % 2
            for cg in range(4):
                ptile, pttag = PTS[cg % 2]
                p.op("T", [
                    (lambda t, k=k, c=8 * cg + j, j=j, ptile=ptile: t.transpose(
                        ptile[:, 128 * j:128 * (j + 1)],
                        A.view("R2", 0, 128, c * 1024 + 128 * k, 128),
                        kident[:, :]))
                    for j in range(8)],
                    reads=[tg for j in range(8) for tg in A.tags("R2", (8 * cg + j) * 1024 + 128 * k, 128)] + ["kident"],
                    writes=[pttag])
                p.op("V", lambda v, kb=kb, cg=cg, ptile=ptile: v.tensor_copy(
                    A.view("stag", 0, 128, kb * 4096 + cg * 1024, 1024), ptile[:, :]),
                    reads=[pttag], writes=A.tags("stag", kb * 4096 + cg * 1024, 1024))
            p.op("G", lambda g, k=k, kb=kb: g.dma_start(
                out=wexp_s[:, :, k, :].rearrange("m r f -> r m f"),
                in_=A.view("stag", 0, 128, kb * 4096, 4096).rearrange("r (m f) -> r m f", f=128)),
                reads=A.tags("stag", kb * 4096, 4096), writes=[f"dram_we:{k}"], dma=f"dse{kb}")

        p.op("G", lambda g: g.dma_start(out=Z.view("fscr", 0, 1, 0, 4096), in_=up_b_in.unsqueeze(0)),
             writes=Z.tags("fscr"), dma="dcf")
        p.op("V", lambda v: v.tensor_copy(Z.view("fident"), kident[:, :]),
             reads=["kident"], writes=Z.tags("fident"))
        for cg in range(4):
            p.op("T", [
                (lambda t, c=8 * cg + j, j=j: t.transpose(
                    PS[5][:, j:j + 1], Z.view("fscr", 0, 1, 128 * c, 128), Z.view("fident", 0, 1, 0, 1)))
                for j in range(8)],
                reads=Z.tags("fscr") + Z.tags("fident"), writes=["P:5"])
            p.op("V", lambda v, cg=cg: v.tensor_tensor(
                bias1[:, 8 * cg:8 * cg + 8], PS[5][:, 0:8],
                A.view("wtb", 0, 128, 8 * cg, 8), ALU.add),
                reads=["P:5"] + A.tags("wtb"), writes=["bias1"])

        for c in range(32):
            p.op("G", lambda g, c=c: g.dma_start(
                out=A.view("R2", 0, 128, c * 1024, 1024),
                in_=w_con_in.rearrange("(c p) d -> c p d", p=128)[c]),
                writes=A.tags("R2", c * 1024, 1024), dma=f"drc{c % 2}")
        high_stages(lambda c: A.tags("R2", c * 1024, 1024),
                    lambda c: A.view("R2", 0, 128, c * 1024, 1024),
                    "hcoefD", transposed=True, reverse=True, width=1024)
        build_T(dn_bf, "dram_dbf", transposed=True, reverse=True)
        for c in range(32):
            for half in range(2):
                pv = PS[2 + (c * 2 + half) % 4]
                ptag = f"P:{2 + (c * 2 + half) % 4}"
                p.op("T", lambda t, c=c, half=half, pv=pv: t.matmul(
                    pv[:, :], A.view("tbuf", 0, 128, 128 * c, 128),
                    A.view("R2", 0, 128, c * 1024 + half * 512, 512),
                    start=True, stop=True),
                    reads=A.tags("tbuf", 128 * c, 128) + A.tags("R2", c * 1024 + half * 512, 512),
                    writes=[ptag])
                p.op("S", lambda s_, c=c, half=half, pv=pv: s_.copy(
                    A.view("R2", 0, 128, c * 1024 + half * 512, 512), pv[:, :]),
                    reads=[ptag], writes=A.tags("R2", c * 1024 + half * 512, 512))
        for c in range(32):
            p.op("G", lambda g, c=c: g.dma_start(
                out=wcon_s[c], in_=A.view("R2", 0, 128, c * 1024, 1024)),
                reads=A.tags("R2", c * 1024, 1024), writes=[f"dram_wc:{c}"], dma=f"dsc{c % 2}")

        def bias2_setup():
            p.op("G", lambda g: g.dma_start(out=Z.view("fscr", 0, 1, 0, 1024), in_=b_con_in.unsqueeze(0)),
                 writes=Z.tags("fscr"), dma="dcf")
            p.op("G", lambda g: g.dma_start(out=Z.view("fscr", 0, 1, 1024, 1024), in_=dn_b_in.unsqueeze(0)),
                 writes=Z.tags("fscr"), dma="dcf")
            p.op("V", lambda v: v.tensor_tensor(
                rowp(0, 0, 1024), Z.view("fscr", 0, 1, 0, 1024),
                Z.view("fscr", 0, 1, 1024, 1024), ALU.add),
                reads=Z.tags("fscr"), writes=A.tags("rowp"))

        bias2_setup()

        def newton_rsqrt(scalar_ok=False):
            if scalar_ok:
                p.op("S", lambda s_: s_.activation(nsc[0][:, 0:4], stats[:, 4:8], AF.Ln,
                                                   bias=keps[:, 0:1], scale=1.0),
                     reads=["stats", "keps"], writes=["nsc0"])
                p.op("S", lambda s_: s_.activation(stats[:, 4:8], nsc[0][:, 0:4], AF.Exp, scale=-0.5),
                     reads=["nsc0"], writes=["stats"])
                return
            v4 = lambda t: t[:, 0:4]
            p.op("V", lambda v: v.tensor_scalar(v4(nsc[0]), stats[:, 4:8], EPS, None, ALU.add),
                 reads=["stats"], writes=["nsc0"])
            p.op("V", lambda v: v.tensor_single_scalar(
                v4(nsci), v4(nsc[0]).bitcast(I32), 1, ALU.logical_shift_right),
                reads=["nsc0"], writes=["nsci"])
            p.op("V", lambda v: v.tensor_scalar(
                v4(nsci), v4(nsci), -1, 0x5f3759df, ALU.mult, ALU.add),
                reads=["nsci"], writes=["nsci"])
            p.op("V", lambda v: v.tensor_copy(v4(nsc[1]), v4(nsci).bitcast(F32)),
                 reads=["nsci"], writes=["nsc1"])
            for _ in range(3):
                p.op("V", lambda v: v.tensor_tensor(v4(nsc[2]), v4(nsc[1]), v4(nsc[1]), ALU.mult),
                     reads=["nsc1"], writes=["nsc2"])
                p.op("V", lambda v: v.tensor_tensor(v4(nsc[2]), v4(nsc[2]), v4(nsc[0]), ALU.mult),
                     reads=["nsc2", "nsc0"], writes=["nsc2"])
                p.op("V", lambda v: v.tensor_scalar(v4(nsc[2]), v4(nsc[2]), -0.5, 1.5, ALU.mult, ALU.add),
                     reads=["nsc2"], writes=["nsc2"])
                p.op("V", lambda v: v.tensor_tensor(v4(nsc[1]), v4(nsc[1]), v4(nsc[2]), ALU.mult),
                     reads=["nsc1", "nsc2"], writes=["nsc1"])
            p.op("V", lambda v: v.tensor_copy(stats[:, 4:8], v4(nsc[1])),
                 reads=["nsc1"], writes=["stats"])

        def emit_ln_tr(xsrc_view, xsrc_tags, xslots, xnT_name, rsqrt_scalar=False):
            nslots = len(xslots)
            for t in range(4):
                p.op("V", lambda v, t=t: v.bn_stats(nsc[3][:, 0:6], xsrc_view(1024 * t, 512)),
                     reads=xsrc_tags(1024 * t, 512), writes=["nsc3"])
                p.op("V", lambda v, t=t: v.bn_stats(nsc[3][:, 6:12], xsrc_view(1024 * t + 512, 512)),
                     reads=xsrc_tags(1024 * t + 512, 512), writes=["nsc3"])
                p.op("V", lambda v, t=t: v.bn_aggr(
                    stats[:, :].rearrange("p (a b) -> p a b", b=4)[:, :, t], nsc[3][:, 0:12]),
                    reads=["nsc3"], writes=["stats"])
            newton_rsqrt(scalar_ok=rsqrt_scalar)

            def emit_norm(t):
                nm, sl = xslots[t % nslots]
                p.op("V", lambda v, t=t, nm=nm, sl=sl: v.tensor_scalar(
                    A.view(nm, 0, 128, sl, 1024), xsrc_view(1024 * t, 1024),
                    stats[:, t:t + 1], stats[:, 4 + t:5 + t], ALU.subtract, ALU.mult),
                    reads=xsrc_tags(1024 * t, 1024) + ["stats"],
                    writes=A.tags(nm, sl, 1024))

            def emit_tr(t):
                nm, sl = xslots[t % nslots]
                ptile, pttag = PTS[t % 2]
                p.op("T", [
                    (lambda tt, t=t, nm=nm, sl=sl, c=c, ptile=ptile: tt.transpose(
                        ptile[:, 128 * c:128 * (c + 1)],
                        A.view(nm, 0, 128, sl + 128 * c, 128), kident[:, :]))
                    for c in range(8)],
                    reads=A.tags(nm, sl, 1024) + ["kident"], writes=[pttag])
                p.op("V", lambda e, t=t, xnT_name=xnT_name, ptile=ptile: e.tensor_copy(
                    A.view(xnT_name).rearrange("p (c q) -> p c q", q=512)[:, :, 128 * t:128 * (t + 1)],
                    ptile[:, :].rearrange("p (c q) -> p c q", q=128)),
                    reads=[pttag],
                    writes=[tg for c in range(8) for tg in A.tags(xnT_name, 512 * c + 128 * t, 128)])

            if nslots >= 4:
                for t in range(4):
                    emit_norm(t)
                for t in range(4):
                    emit_tr(t)
            else:
                for t in range(4):
                    emit_norm(t)
                    emit_tr(t)

        def emit_ffn(src_d, dst_d, tag_pfx):
            def load_x(s):
                sb = s % 2
                p.op("Y", lambda y, s=s, sb=sb: y.dma_start(
                    out=Z.view("xb", 0, 128, sb * 4096, 4096).rearrange("p (t f) -> p t f", f=1024),
                    in_=src_d[SUP * s:SUP * (s + 1)].rearrange("(t p) f -> p t f", p=128)),
                    writes=Z.tags("xb", sb * 4096, 4096), dma=f"dx{sb}")

            def ln_super(s):
                sb = s % 2
                xv_ = lambda c0, cn, sb=sb: Z.view("xb", 0, 128, sb * 4096 + c0, cn)
                xt_ = lambda c0, cn, sb=sb: Z.tags("xb", sb * 4096 + c0, cn)
                emit_ln_tr(xv_, xt_, [("xn", i * 1024) for i in range(4)],
                           "xnT" if s % 2 == 0 else "xnT2")
                return xv_, xt_

            load_x(0)
            xv, xt = ln_super(0)
            for s in range(NSUP):
                sb = s % 2
                xnT_nm = "xnT" if s % 2 == 0 else "xnT2"
                if s + 1 < NSUP:
                    load_x(s + 1)

                def wes_load(qd):
                    sl = (4 * qd) % 8
                    p.op("G", lambda g, qd=qd, sl=sl: g.dma_start(
                        out=A.view("wes", 0, 128, sl * 1024, 4096).rearrange("r (m kf) -> r m kf", kf=1024),
                        in_=wexp_s[4 * qd:4 * qd + 4].rearrange("m r k f -> r m (k f)")),
                        reads=[f"dram_we:{k}" for k in range(8)],
                        writes=A.tags("wes", sl * 1024, 4096), dma=f"dwe{qd % 2}")

                wes_load(0)
                wes_load(1)

                for m in range(32):
                    pv, ptag = PS[m % 2], f"P:{m % 2}"
                    p.op("T", [
                        (lambda t, m=m, k=k, pv=pv, xnT_nm=xnT_nm: t.matmul(
                            pv[:, :],
                            A.view("wes", 0, 128, (m % 8) * 1024 + 128 * k, 128),
                            A.view(xnT_nm, 0, 128, 512 * k, 512),
                            start=(k == 0), stop=(k == 7)))
                        for k in range(8)],
                        reads=A.tags("wes", (m % 8) * 1024, 1024) + A.tags(xnT_nm),
                        writes=[ptag])
                    if m % 4 == 3 and (m + 5) // 4 < 8:
                        wes_load((m + 5) // 4)
                    p.op("S", lambda s_, m=m, pv=pv: s_.activation(
                        A.view("hh", 0, 128, 512 * m, 512), pv[:, :], AF.Gelu,
                        bias=bias1[:, m:m + 1], scale=1.0),
                        reads=[ptag, "bias1"], writes=A.tags("hh", 512 * m, 512))

                xv_cur, xt_cur = xv, xt
                if s + 1 < NSUP:
                    xv, xt = ln_super(s + 1)

                for t in range(4):
                    for half in range(2):
                        pv = PS[2 + (t * 2 + half) % 2]
                        ptag = f"P:{2 + (t * 2 + half) % 2}"
                        blds = [
                            (lambda tt, k=k, t=t, half=half, pv=pv: tt.matmul(
                                pv[:, :],
                                A.view("hh", 0, 128, 512 * k + 128 * t, 128),
                                A.view("R2", 0, 128, 1024 * k + 512 * half, 512),
                                start=(k == 0), stop=False))
                            for k in range(32)]
                        blds.append(lambda tt, half=half, pv=pv: tt.matmul(
                            pv[:, :], kones[0:1, 0:128], rowp(0, half * 512, 512),
                            start=False, stop=True))
                        p.op("T", blds,
                             reads=A.tags("hh") + A.tags("R2") + ["kones"] + A.tags("rowp"),
                             writes=[ptag])
                        yb = t % 2
                        p.op("V", lambda v, t=t, half=half, pv=pv, yb=yb, xv=xv_cur: v.scalar_tensor_tensor(
                            Z.view("yb", 0, 128, yb * 1024 + half * 512, 512),
                            pv[:, :], 0.5, xv(1024 * t + 512 * half, 512),
                            ALU.mult, ALU.add),
                            reads=[ptag] + xt_cur(1024 * t + 512 * half, 512),
                            writes=Z.tags("yb", yb * 1024 + half * 512, 512))
                    yb = t % 2
                    eng = "G" if t % 2 == 0 else "Y"
                    p.op(eng, lambda e, s=s, t=t, yb=yb: e.dma_start(
                        out=dst_d[SUP * s + 128 * t:SUP * s + 128 * (t + 1)],
                        in_=Z.view("yb", 0, 128, yb * 1024, 1024)),
                        reads=Z.tags("yb", yb * 1024, 1024), dma=f"dy{yb}")

        emit_ffn(x_in, x1_d, "f1")
        pack_qk()
        p.fence()

        for c in range(8):
            p.op("G", lambda g, c=c: g.dma_start(
                out=A.view("wv", 0, 128, 1024 * c, 1024).rearrange("p (h d) -> p h d", d=64),
                in_=wv_in.rearrange("h (c p) d -> c p h d", p=128)[c]),
                writes=A.tags("wv", 1024 * c, 1024), dma="dwv")
        for c in range(8):
            p.op("G", lambda g, c=c: g.dma_start(
                out=A.view("wos", 0, 128, 1024 * c, 1024),
                in_=wo_in.rearrange("(c p) d -> c p d", p=128)[c]),
                writes=A.tags("wos", 1024 * c, 1024), dma="dwo")
        p.op("G", lambda g: g.dma_start(out=kbrow(0, 1024),
                                        in_=bq_in.rearrange("h d -> (h d)").unsqueeze(0)),
             writes=kbt, dma="dcf")
        p.op("G", lambda g: g.dma_start(out=kbrow(1024, 1024),
                                        in_=bk_in.rearrange("h d -> (h d)").unsqueeze(0)),
             writes=kbt, dma="dcf")
        p.op("G", lambda g: g.dma_start(out=kbrow(2048, 1024),
                                        in_=bv_in.rearrange("h d -> (h d)").unsqueeze(0)),
             writes=kbt, dma="dcf")
        p.op("G", lambda g: g.dma_start(out=kbrow(3072, 1024), in_=bo_in.unsqueeze(0)),
             writes=kbt, dma="dcf")
        p.op("V", lambda v: v.memset(
            A.view("vs").rearrange("p (t h d) -> p t h d", t=16, d=65)[:, :, :, 64], 1.0),
            writes=A.tags("vs"))

        for q in range(NSEQ):
            base = q * SEQ
            xnT_v = lambda c0, cn: A.view("axnT", 0, 128, c0, cn)
            xnT_t = A.tags("axnT")
            PTf, PT2f = PT.bitcast(F32), PT2.bitcast(F32)

            def dxa_issue(u, base=base):
                xo = (u % 2) * 4096
                p.op("Y", lambda y, base=base, u=u, xo=xo: y.dma_start(
                    out=Z.view("xa", 0, 128, xo, 4096).rearrange("p (t f) -> p t f", f=1024),
                    in_=x1_d[base + SUP * u: base + SUP * (u + 1)].rearrange("(t p) f -> p t f", p=128)),
                    writes=Z.tags("xa", xo, 4096), dma=f"dxa{u % 2}")

            def ln_u(u):
                xo = (u % 2) * 4096
                xav = lambda c0, cn, xo=xo: Z.view("xa", 0, 128, xo + c0, cn)
                xat = lambda c0, cn, xo=xo: Z.tags("xa", xo + c0, cn)
                emit_ln_tr(xav, xat, [("axn", 0), ("axn", 1024), ("axn2", 0), ("axn2", 1024)],
                           "axnT", rsqrt_scalar=True)

            def wqk_load(pi):
                sl = (pi % 2) * 2048
                dram, nm = (wqd, "wq") if pi < 4 else (wkd, "wk")
                pr0 = (pi % 4) * 2
                p.op("G", lambda g, dram=dram, pr0=pr0, sl=sl: g.dma_start(
                    out=A.view("wqk", 0, 128, sl, 2048).rearrange("p (w cd) -> p w cd", cd=1024),
                    in_=dram[pr0:pr0 + 2].rearrange("w p c d -> p w (c d)")),
                    reads=[f"dram_{nm}:{pr}:{hh}" for pr in (pr0, pr0 + 1) for hh in (0, 1)],
                    writes=A.tags("wqk", sl, 2048), dma=f"dwqk{pi % 2}")

            def qk_group(pi, pp, uu):
                isk = pi >= 4
                bcol = 1024 if isk else 0
                pr = (pi % 4) * 2 + pp
                sl = (pi % 2) * 2048 + pp * 1024
                if isk:
                    pv, ptag = (PTf, "P:T") if pr % 2 == 0 else (PT2f, "P:T2")
                else:
                    pv, ptag = PS[pr % 2], f"P:{pr % 2}"
                blds = [
                    (lambda t, sl=sl, k=k, pv=pv: t.matmul(
                        pv[:, :],
                        A.view("wqk", 0, 128, sl + 128 * k, 128),
                        xnT_v(512 * k, 512), start=(k == 0), stop=False))
                    for k in range(8)]
                blds.append(lambda t, bcol=bcol, pr=pr, pv=pv: t.matmul(
                    pv[:, :], kbrow(bcol + 128 * pr, 128),
                    kones[0:1, 0:512], start=False, stop=True))
                p.op("T", blds,
                     reads=A.tags("wqk", sl, 1024) + xnT_t + kbt + ["kones"], writes=[ptag])
                if isk:
                    p.op("V", lambda v, pr=pr, uu=uu, pv=pv: v.tensor_copy(
                        A.view("ks", 0, 128, 2048 * pr + 512 * uu, 512), pv[:, :]),
                        reads=[ptag], writes=A.tags("ks", 2048 * pr + 512 * uu, 512))
                else:
                    p.op("S", lambda s_, pr=pr, pv=pv: s_.copy(
                        A.view("qs", 0, 128, 512 * pr, 512), pv[:, :]),
                        reads=[ptag], writes=A.tags("qs", 512 * pr, 512))

            def v_group(t, half, uu):
                pv, ptag = ((PTf, "P:T") if half == 0 else (PT2f, "P:T2"))
                blds = [
                    (lambda tt, t=t, half=half, k=k, pv=pv: tt.matmul(
                        pv[:, :], xnT_v(512 * k + 128 * t, 128),
                        A.view("wv", 0, 128, 1024 * k + 512 * half, 512),
                        start=(k == 0), stop=False))
                    for k in range(8)]
                blds.append(lambda tt, half=half, pv=pv: tt.matmul(
                    pv[:, :], kones[0:1, 0:128],
                    kbrow(2048 + half * 512, 512),
                    start=False, stop=True))
                p.op("T", blds,
                     reads=A.tags("wv") + xnT_t + kbt + ["kones"], writes=[ptag])
                p.op("V", lambda v, uu=uu, t=t, half=half, pv=pv: v.tensor_copy(
                    A.view("vs", 0, 128, 1040 * (4 * uu + t) + 520 * half, 520
                           ).rearrange("p (h d) -> p h d", d=65)[:, :, 0:64],
                    pv[:, :].rearrange("p (h d) -> p h d", d=64)),
                    reads=[ptag],
                    writes=A.tags("vs", 1040 * (4 * uu + t) + 520 * half, 520))

            def kv_items(uu):
                items = [lambda: wqk_load(4), lambda: wqk_load(5)]
                for pi in (4, 5, 6, 7):
                    for pp in (0, 1):
                        items.append(lambda pi=pi, pp=pp: qk_group(pi, pp, uu))
                    if pi + 2 < 8:
                        items.append(lambda pi=pi: wqk_load(pi + 2))
                for t in range(4):
                    for half in range(2):
                        items.append(lambda t=t, half=half: v_group(t, half, uu))
                return items

            dxa_issue(0)
            dxa_issue(1)
            ln_u(0)
            for u in range(4):
                xo = (u % 2) * 4096
                wqk_load(0)
                wqk_load(1)
                for pi in range(4):
                    for pp in range(2):
                        qk_group(pi, pp, u)
                    if pi + 2 < 4:
                        wqk_load(pi + 2)
                if u == 0:
                    for it in kv_items(0):
                        it()
                for mat, dstt, dsttag, fresh, off in (
                        ("qs", qmax, "qmax", True, lambda pr: 512 * pr),
                        ("ks", kqmax, "kqmax", u == 0, lambda pr, u=u: 2048 * pr + 512 * u)):
                    def nsq_sq(pr, mat=mat, off=off):
                        so = (pr % 4) * 512
                        p.op("V", lambda v, mat=mat, pr=pr, so=so, off=off: v.tensor_tensor(
                            A.view("nsq", 0, 128, so, 512), A.view(mat, 0, 128, off(pr), 512),
                            A.view(mat, 0, 128, off(pr), 512), ALU.mult),
                            reads=A.tags(mat, off(pr), 512), writes=A.tags("nsq", so, 512))

                    for pr in range(4):
                        nsq_sq(pr)
                    for pr in range(8):
                        so = (pr % 4) * 512
                        pv, ptag = PS[4 + pr % 2], f"P:{4 + pr % 2}"
                        p.op("T", lambda t, so=so, pv=pv: t.matmul(
                            pv[:, :], khead[:, :], A.view("nsq", 0, 128, so, 512),
                            start=True, stop=True),
                            reads=A.tags("nsq", so, 512) + ["khead"], writes=[ptag])
                        if pr + 4 < 8:
                            nsq_sq(pr + 4)
                        if fresh:
                            p.op("V", lambda v, dstt=dstt, pr=pr, pv=pv: v.tensor_reduce(
                                dstt[:, pr:pr + 1], pv[:, :], AX.X, ALU.max),
                                reads=[ptag], writes=[dsttag])
                        else:
                            p.op("V", lambda v, pv=pv: v.tensor_reduce(
                                ctmp[0][:, 0:1], pv[:, :], AX.X, ALU.max),
                                reads=[ptag], writes=["ctmp0"])
                            p.op("V", lambda v, dstt=dstt, pr=pr: v.tensor_tensor(
                                dstt[:, pr:pr + 1], dstt[:, pr:pr + 1], ctmp[0][:, 0:1], ALU.max),
                                reads=[dsttag, "ctmp0"], writes=[dsttag])
                p.op("S", lambda s_: s_.activation(ctmp[0][:, 0:8], qmax[:, 0:8], AF.Ln),
                     reads=["qmax"], writes=["ctmp0"])
                p.op("S", lambda s_: s_.activation(ctmp[1][:, 0:8], kqmax[:, 0:8], AF.Ln),
                     reads=["kqmax"], writes=["ctmp1"])
                p.op("V", lambda v: v.tensor_tensor(ctmp[0][:, 0:8], ctmp[0][:, 0:8], ctmp[1][:, 0:8], ALU.add),
                     reads=["ctmp0", "ctmp1"], writes=["ctmp0"])
                p.op("S", lambda s_: s_.activation(ctmp[1][:, 0:8], ctmp[0][:, 0:8], AF.Exp, scale=0.5),
                     reads=["ctmp0"], writes=["ctmp1"])
                p.op("V", lambda v: v.tensor_scalar(negc8[:, 0:8], ctmp[1][:, 0:8], -0.125, None, ALU.mult),
                     reads=["ctmp1"], writes=["negc8"])
                for hf in range(2):
                    pv, ptag = PS[4 + hf], f"P:{4 + hf}"
                    p.op("T", lambda t, hf=hf, pv=pv: t.matmul(
                        pv[:, 0:8], kones[64 * hf:64 * hf + 1, 0:128],
                        negc8[64 * hf:64 * hf + 1, 0:8], start=True, stop=True),
                        reads=["negc8", "kones"], writes=[ptag])
                    p.op("V", lambda v, hf=hf, pv=pv: v.tensor_copy(
                        negc.rearrange("p (a b) -> p a b", b=2)[:, :, hf], pv[:, 0:8]),
                        reads=[ptag], writes=["negc"])

                if u < 3:
                    ln_u(u + 1)
                    fill = kv_items(u + 1)
                else:
                    fill = []

                srow64_v = lambda: Z.view("srow", 64, 1, 0, 512)
                srowR64_v = lambda: Z.view("srowRF", 64, 1, 0, 256).bitcast(BF16)

                def tail_a(po, otag):
                    p.op("S", lambda s_, po=po: s_.activation(
                        srow64_v(), po[64:65, :], AF.Ln),
                        reads=[otag], writes=Z.tags("srow"))

                    def _rexp(s_):
                        with nc.allow_low_precision(reason="bf16 softmax recip (exp(-ln(sums))) validated in golden model"):
                            return s_.activation(srowR64_v(), srow64_v(), AF.Exp, scale=-1.0)
                    p.op("S", _rexp,
                         reads=Z.tags("srow"), writes=Z.tags("srowRF"))

                def tail_b(pr, hf, po, otag):
                    p.op("T", lambda t: t.matmul(
                        PS[4][0:64, :], kones[64:65, 0:64], srowR64_v(), start=True, stop=True),
                        reads=Z.tags("srowRF") + ["kones"], writes=["P:4"])
                    p.op("V", lambda v: v.tensor_copy(srep_v(), PS[4][0:64, :]),
                         reads=["P:4"], writes=Z.tags("srep"))
                    p.op("V", lambda v, pr=pr, hf=hf, po=po, otag=otag: v.tensor_tensor(
                        A.view("os", 64 * hf, 64, 512 * pr, 512), po[0:64, :],
                        srep_v(), ALU.mult),
                        reads=[otag] + Z.tags("srep"), writes=A.tags("os", 512 * pr, 512))

                pend_tail = None
                for h in range(NH):
                    pr, hf = h // 2, h % 2
                    kt = 4 * (u + 1)
                    po, otag = PS[2 + h % 2], f"P:{2 + h % 2}"

                    def pv_op(i, h=h, kt=kt, po=po, otag=otag):
                        pair = i // 2
                        pnm, poff = PBS[pair % 3]
                        pcol = poff + 512 * (i % 2)
                        p.op("T", lambda t, i=i, h=h, kt=kt, po=po, pnm=pnm, pcol=pcol: t.matmul(
                            po[0:65, :],
                            A.view("vs", 0, 128, 1040 * i + 65 * h, 65),
                            A.view(pnm, 0, 128, pcol, 512),
                            start=(i == 0), stop=(i == kt - 1)),
                            reads=A.tags("vs", 1040 * i + 65 * h, 65) + A.tags(pnm, pcol, 512),
                            writes=[otag], accum_tags=(otag,))

                    for pair in range(kt // 2):
                        bs = pair % 2
                        for ih in range(2):
                            i = 2 * pair + ih
                            pv, ptag = (PS[ih], f"P:{ih}") if bs == 0 else (PS[4 + ih], f"P:{4 + ih}")
                            p.op("T", lambda t, pr=pr, hf=hf, i=i, pv=pv: t.matmul(
                                pv[:, :],
                                A.view("ks", 64 * hf, 64, 2048 * pr + 128 * i, 128),
                                A.view("qs", 64 * hf, 64, 512 * pr, 512),
                                start=True, stop=True),
                                reads=A.tags("ks", 2048 * pr, 512 * (u + 1)) + A.tags("qs", 512 * pr, 512),
                                writes=[ptag])
                        pset = ps01 if bs == 0 else ps45
                        ptags = ["P:0", "P:1"] if bs == 0 else ["P:4", "P:5"]
                        pnm, poff = PBS[pair % 3]
                        p.op("S", lambda s_, h=h, pnm=pnm, poff=poff, pset=pset: s_.activation(
                            A.view(pnm, 0, 128, poff, 1024), pset[:, :], AF.Exp,
                            bias=negc[:, h:h + 1], scale=0.125),
                            reads=ptags + ["negc"],
                            writes=A.tags(pnm, poff, 1024))
                        for ih in range(2):
                            i = 2 * pair + ih
                            if i >= 4 * u:
                                d = i - 4 * u
                                p.op("V", lambda v, pnm=pnm, poff=poff, ih=ih, d=d: v.tensor_tensor(
                                    A.view(pnm, 0, 128, poff + 512 * ih, 512),
                                    A.view(pnm, 0, 128, poff + 512 * ih, 512),
                                    kmask01[:, 512 - 128 * d:1024 - 128 * d], ALU.mult),
                                    reads=A.tags(pnm, poff + 512 * ih, 512) + ["kmask01"],
                                    writes=A.tags(pnm, poff + 512 * ih, 512))
                        if pair == 0 and pend_tail is not None:
                            tail_b(*pend_tail)
                            pend_tail = None
                        if pair >= 2:
                            pv_op(2 * pair - 4)
                            pv_op(2 * pair - 3)
                    if fill:
                        take = (len(fill) + NH - 1 - h) // (NH - h)
                        for _ in range(take):
                            fill.pop(0)()
                    for i in range(max(kt - 4, 0), kt):
                        pv_op(i)
                    tail_a(po, otag)
                    pend_tail = (pr, hf, po, otag)
                while fill:
                    fill.pop(0)()
                tail_b(*pend_tail)
                for half in range(2):
                    for c in range(8):
                        for t in range(4):
                            pv, ptag = PS[2 + t], f"P:{2 + t}"
                            p.op("T", lambda tt, c=c, t=t, half=half, pv=pv: tt.matmul(
                                pv[:, :],
                                A.view("os", 0, 128, 512 * c + 128 * t, 128),
                                A.view("wos", 0, 128, 1024 * c + 512 * half, 512),
                                start=(c == 0), stop=False),
                                reads=A.tags("os", 512 * c, 512) + A.tags("wos", 1024 * c + 512 * half, 512),
                                writes=[ptag], accum_tags=(ptag,))
                    for t in range(4):
                        pv, ptag = PS[2 + t], f"P:{2 + t}"
                        p.op("T", lambda tt, half=half, pv=pv: tt.matmul(
                            pv[:, :], kones[0:1, 0:128],
                            kbrow(3072 + half * 512, 512),
                            start=False, stop=True),
                            reads=["kones"] + kbt, writes=[ptag], accum_tags=(ptag,))
                        p.op("V", lambda v, t=t, half=half, pv=pv, xo=xo: v.tensor_tensor(
                            Z.view("x2b", 0, 128, t * 512, 512), pv[:, :],
                            Z.view("xa", 0, 128, xo + t * 1024 + 512 * half, 512), ALU.add),
                            reads=[ptag] + Z.tags("xa", xo + t * 1024 + 512 * half, 512),
                            writes=Z.tags("x2b", t * 512, 512))
                        p.op("G", lambda g, base=base, u=u, t=t, half=half: g.dma_start(
                            out=x2_d[base + 512 * u + 128 * t: base + 512 * u + 128 * (t + 1),
                                     512 * half:512 * (half + 1)],
                            in_=Z.view("x2b", 0, 128, t * 512, 512)),
                            reads=Z.tags("x2b", t * 512, 512), dma=f"dx2{t % 2}")
                if u + 2 <= 3:
                    dxa_issue(u + 2)
        p.fence()

        for c in range(32):
            p.op("G", lambda g, c=c: g.dma_start(
                out=A.view("R2", 0, 128, c * 1024, 1024), in_=wcon_s[c]),
                reads=[f"dram_wc:{c}"],
                writes=A.tags("R2", c * 1024, 1024), dma=f"drc{c % 2}")
        bias2_setup()
        emit_ffn(x2_d, y_out, "f2")
        p.fence()

        p.emit()
    return nc


_NC_CACHE = {}


def _get_nc():
    if "nc" not in _NC_CACHE:
        _NC_CACHE["nc"] = build_nc()
    return _NC_CACHE["nc"]


def _consts():
    ident = np.eye(128, dtype=BFD)
    tident = np.tile(np.eye(128, dtype=np.float32), (1, 32)).astype(BFD)
    pp = np.arange(128)[:, None]
    xx = np.arange(1024)[None, :]
    bigmask = np.where(pp <= xx - 512, 0.0, -30000.0).astype(BFD)
    mask01 = np.where(pp <= xx - 512, 1.0, 0.0).astype(BFD)
    ho = (np.arange(128)[:, None] // 64 == np.arange(128)[None, :] // 64)
    headones = ho.astype(BFD)
    return dict(cident=ident, ctident=tident, cbigmask=bigmask, cmask01=mask01,
                cheadones=headones)


def kernel(**inputs):
    x = np.ascontiguousarray(np.asarray(inputs["L0_attn_x"], np.float32))
    shards = x.reshape(NCORE, TOK, D)
    consts = _consts()
    weights = {nm: np.ascontiguousarray(np.asarray(inputs[nm], np.float32)) for nm in WNAMES}
    nc = _get_nc()
    in_maps = []
    for i in range(NCORE):
        m = {"x": shards[i]}
        m.update(weights)
        m.update(consts)
        in_maps.append(m)
    res = run_bass_kernel_spmd(nc, in_maps, core_ids=list(range(NCORE)))
    out = np.concatenate([np.asarray(r["y"], np.float32).reshape(NSEQ, SEQ, D)
                          for r in res.results], axis=0)
    return out.reshape(32, SEQ, D)


# version 21
# speedup vs baseline: 1.0279x; 1.0279x over previous
import contextlib
from collections import defaultdict

import numpy as np
import ml_dtypes

import concourse.bass as bass
import concourse.mybir as mybir
from concourse.bass_utils import run_bass_kernel_spmd

F32 = mybir.dt.float32
BF16 = mybir.dt.bfloat16
I32 = mybir.dt.int32
AF = mybir.ActivationFunctionType
ALU = mybir.AluOpType
AX = mybir.AxisListType
BFD = ml_dtypes.bfloat16

NCORE = 8
SEQ = 2048
NSEQ = 4
D = 1024
HDIM = 4096
NH = 16
TOK = NSEQ * SEQ
SUP = 512
NSUP = TOK // SUP
EPS = 1e-5
RS = float(2 ** 24)
RSI = float(2 ** -24)

WNAMES = [
    "L0_attn_Wq", "L0_attn_bq", "L0_attn_Wk", "L0_attn_bk", "L0_attn_Wv",
    "L0_attn_bv", "L0_attn_Wo", "L0_attn_bo", "L0_ffn_W_exp", "L0_ffn_b_exp",
    "L0_ffn_up_weights", "L0_ffn_up_bias", "L0_ffn_down_weights",
    "L0_ffn_W_con", "L0_ffn_b_con", "L0_ffn_down_bias",
]


class Prog:
    ENGINES = ("T", "V", "S", "G", "Y")

    def __init__(self, nc, stack):
        self.nc = nc
        self.stack = stack
        self.ops = []
        self.tag_w = {}
        self.tag_r = defaultdict(list)
        self.clock = {e: {} for e in self.ENGINES}
        self.op_clock = []
        self.dma_count = defaultdict(int)
        self.dma_clock = {}
        self.sems = {}

    def _sem(self, name):
        if name not in self.sems:
            self.sems[name] = self.stack.enter_context(self.nc.semaphore(name))
        return self.sems[name]

    @staticmethod
    def _join(dst, src):
        for k, v in src.items():
            if dst.get(k, -1) < v:
                dst[k] = v

    def _covered(self, clk, m):
        if m[0] == "e":
            return clk.get(("e", m[1]), -1) >= m[2]
        return clk.get(("d", m[1]), -1) >= m[2]

    def op(self, engine, builders, reads=(), writes=(), dma=None, accum_tags=()):
        if callable(builders):
            builders = [builders]
        idx = len(self.ops)
        e = engine
        waits = []

        def need(m):
            if m is None:
                return
            if m[0] == "d":
                m = ("d", m[1], self.dma_count[m[1]] * 16)
            if not self._covered(self.clock[e], m):
                if m not in waits:
                    waits.append(m)

        reads = list(reads)
        writes = list(writes)
        for t in reads:
            w = self.tag_w.get(t)
            if w is not None:
                if w[0] == "e" and w[1] == e and t in accum_tags:
                    pass
                else:
                    need(w)
        for t in writes:
            w = self.tag_w.get(t)
            if w is not None and not (w[0] == "e" and w[1] == e and t in accum_tags):
                need(w)
            for r in self.tag_r[t]:
                if r[0] == "e" and r[1] == e:
                    continue
                need(r)
        for t in reads:
            if t.startswith("P:") and t not in writes:
                for r in self.tag_r[t]:
                    if r[0] == "e" and r[1] == e:
                        continue
                    need(r)

        if dma is not None:
            self.dma_count[dma] += 1
            marker = ("d", dma, self.dma_count[dma] * 16)
        else:
            marker = ("e", e, idx)

        myclk = self.clock[e]
        for m in waits:
            if m[0] == "e":
                self._join(myclk, self.op_clock[m[2]])
                myclk[("e", m[1])] = max(myclk.get(("e", m[1]), -1), m[2])
            else:
                self._join(myclk, self.dma_clock[(m[1], m[2])])
                myclk[("d", m[1])] = max(myclk.get(("d", m[1]), -1), m[2])

        snap = dict(myclk)
        if dma is not None:
            snap[("d", dma)] = self.dma_count[dma] * 16
            self.dma_clock[(dma, self.dma_count[dma] * 16)] = snap
        else:
            snap[("e", e)] = idx
        self.op_clock.append(snap)

        for t in writes:
            self.tag_w[t] = marker
            self.tag_r[t] = []
        for t in reads:
            if t not in writes:
                self.tag_r[t].append(marker)

        self.ops.append(dict(engine=e, builders=builders, waits=waits, dma=dma,
                             idx=idx, inc_needed=False))
        return idx

    def fence(self):
        markers = []
        last_eng = {}
        for o in self.ops:
            if o["dma"] is None and o["builders"]:
                last_eng[o["engine"]] = o["idx"]
        for e, i in last_eng.items():
            markers.append(("e", e, i))
        for dsem, cnt in self.dma_count.items():
            markers.append(("d", dsem, cnt * 16))
        for e in self.ENGINES:
            idx = len(self.ops)
            waits = [m for m in markers
                     if not (m[0] == "e" and m[1] == e)
                     and not self._covered(self.clock[e], m)]
            myclk = self.clock[e]
            for m in waits:
                if m[0] == "e":
                    self._join(myclk, self.op_clock[m[2]])
                    myclk[("e", m[1])] = max(myclk.get(("e", m[1]), -1), m[2])
                else:
                    self._join(myclk, self.dma_clock[(m[1], m[2])])
                    myclk[("d", m[1])] = max(myclk.get(("d", m[1]), -1), m[2])
            snap = dict(myclk)
            snap[("e", e)] = idx
            self.op_clock.append(snap)
            self.ops.append(dict(engine=e, builders=[], waits=waits, dma=None,
                                 idx=idx, inc_needed=False))

    def emit(self):
        for o in self.ops:
            for m in o["waits"]:
                if m[0] == "e":
                    self.ops[m[2]]["inc_needed"] = True
        eng_rank = {}
        counters = defaultdict(int)
        for o in self.ops:
            if o["dma"] is None and o["inc_needed"]:
                counters[o["engine"]] += 1
                eng_rank[o["idx"]] = counters[o["engine"]]
        for e in self.ENGINES:
            if counters[e]:
                self._sem(f"sem_{e}")
        for dsem in list(self.dma_count):
            self._sem(dsem)

        per_engine = {e: [] for e in self.ENGINES}
        for o in self.ops:
            per_engine[o["engine"]].append(o)
        block = self.stack.enter_context(self.nc.Block())

        def run_stream(eng_obj, ops):
            for o in ops:
                for m in o["waits"]:
                    if m[0] == "e":
                        eng_obj.wait_ge(self._sem(f"sem_{m[1]}"), eng_rank[m[2]])
                    else:
                        eng_obj.wait_ge(self._sem(m[1]), m[2])
                n = len(o["builders"])
                for bi, b in enumerate(o["builders"]):
                    ins = b(eng_obj)
                    if bi == n - 1:
                        if o["dma"] is not None:
                            ins.then_inc(self._sem(o["dma"]), 16)
                        elif o["inc_needed"]:
                            ins.then_inc(self._sem(f"sem_{o['engine']}"), 1)

        section = {"T": block.tensor, "V": block.vector, "S": block.scalar,
                   "G": block.gpsimd, "Y": block.sync}
        for e in self.ENGINES:
            if per_engine[e]:
                def make(ops=per_engine[e]):
                    return lambda eng_obj: run_stream(eng_obj, ops)
                section[e](make())
        return block


class Arena:
    def __init__(self, tensor, name, cols, blk=2048):
        self.t = tensor
        self.name = name
        self.cols = cols
        self.blk = blk
        self.m = {}

    def place(self, name, off, n):
        assert off + n <= self.cols, f"{self.name}:{name} overflow {off}+{n}>{self.cols}"
        self.m[name] = (off, n)

    def view(self, name, p0=0, pn=128, c0=0, cn=None):
        off, n = self.m[name]
        if cn is None:
            cn = n - c0
        assert c0 + cn <= n, f"view {name} {c0}+{cn}>{n}"
        return self.t[p0:p0 + pn, off + c0: off + c0 + cn]

    def tags(self, name, c0=0, cn=None):
        off, n = self.m[name]
        if cn is None:
            cn = n - c0
        lo = (off + c0) // self.blk
        hi = (off + c0 + cn - 1) // self.blk
        return [f"{self.name}:{i}" for i in range(lo, hi + 1)]


def build_nc():
    nc = bass.Bass("TRN2", target_bir_lowering=False)

    x_in = nc.dram_tensor("x", [TOK, D], F32, kind="ExternalInput").ap()
    y_out = nc.dram_tensor("y", [TOK, D], F32, kind="ExternalOutput").ap()
    w_exp_in = nc.dram_tensor("L0_ffn_W_exp", [D, HDIM], F32, kind="ExternalInput").ap()
    b_exp_in = nc.dram_tensor("L0_ffn_b_exp", [HDIM], F32, kind="ExternalInput").ap()
    up_w_in = nc.dram_tensor("L0_ffn_up_weights", [12, 2048, 2, 2], F32, kind="ExternalInput").ap()
    up_b_in = nc.dram_tensor("L0_ffn_up_bias", [HDIM], F32, kind="ExternalInput").ap()
    dn_w_in = nc.dram_tensor("L0_ffn_down_weights", [12, 2048, 2, 2], F32, kind="ExternalInput").ap()
    w_con_in = nc.dram_tensor("L0_ffn_W_con", [HDIM, D], F32, kind="ExternalInput").ap()
    b_con_in = nc.dram_tensor("L0_ffn_b_con", [D], F32, kind="ExternalInput").ap()
    dn_b_in = nc.dram_tensor("L0_ffn_down_bias", [D], F32, kind="ExternalInput").ap()
    wq_in = nc.dram_tensor("L0_attn_Wq", [NH, D, 64], F32, kind="ExternalInput").ap()
    bq_in = nc.dram_tensor("L0_attn_bq", [NH, 64], F32, kind="ExternalInput").ap()
    wk_in = nc.dram_tensor("L0_attn_Wk", [NH, D, 64], F32, kind="ExternalInput").ap()
    bk_in = nc.dram_tensor("L0_attn_bk", [NH, 64], F32, kind="ExternalInput").ap()
    wv_in = nc.dram_tensor("L0_attn_Wv", [NH, D, 64], F32, kind="ExternalInput").ap()
    bv_in = nc.dram_tensor("L0_attn_bv", [NH, 64], F32, kind="ExternalInput").ap()
    wo_in = nc.dram_tensor("L0_attn_Wo", [D, D], F32, kind="ExternalInput").ap()
    bo_in = nc.dram_tensor("L0_attn_bo", [D], F32, kind="ExternalInput").ap()
    ident_in = nc.dram_tensor("cident", [128, 128], BF16, kind="ExternalInput").ap()
    tident_in = nc.dram_tensor("ctident", [128, HDIM], BF16, kind="ExternalInput").ap()
    bigmask_in = nc.dram_tensor("cbigmask", [128, 1024], BF16, kind="ExternalInput").ap()
    mask01_in = nc.dram_tensor("cmask01", [128, 1024], BF16, kind="ExternalInput").ap()
    headones_in = nc.dram_tensor("cheadones", [128, 128], BF16, kind="ExternalInput").ap()

    x1_d = nc.dram_tensor("x1", [TOK, D], F32, kind="Internal").ap()
    x2_d = nc.dram_tensor("x2", [TOK, D], F32, kind="Internal").ap()
    wexp_s = nc.dram_tensor("wexp_s", [32, 128, 8, 128], BF16, kind="Internal").ap()
    wcon_s = nc.dram_tensor("wcon_s", [32, 128, 1024], BF16, kind="Internal").ap()
    wqd = nc.dram_tensor("wqd", [8, 128, 8, 128], BF16, kind="Internal").ap()
    wkd = nc.dram_tensor("wkd", [8, 128, 8, 128], BF16, kind="Internal").ap()
    up_bf = nc.dram_tensor("up_bf", [12, 2048, 2, 2], BF16, kind="Internal").ap()
    dn_bf = nc.dram_tensor("dn_bf", [12, 2048, 2, 2], BF16, kind="Internal").ap()

    st = contextlib.ExitStack()
    with st:
        kident = st.enter_context(nc.sbuf_tensor("kident", [128, 128], BF16))
        khead = st.enter_context(nc.sbuf_tensor("khead", [128, 128], BF16))
        kmask = st.enter_context(nc.sbuf_tensor("kmask", [128, 1024], BF16))
        kmask01 = st.enter_context(nc.sbuf_tensor("kmask01", [128, 1024], BF16))
        kones = st.enter_context(nc.sbuf_tensor("kones", [128, 512], BF16))
        krsi = st.enter_context(nc.sbuf_tensor("krsi", [1, 64], BF16))
        bias1 = st.enter_context(nc.sbuf_tensor("bias1", [128, 32], F32))
        negc = st.enter_context(nc.sbuf_tensor("negc", [128, 16], F32))
        stats = st.enter_context(nc.sbuf_tensor("stats", [128, 8], F32))
        nsc = [st.enter_context(nc.sbuf_tensor(f"nsc{i}", [128, 12], F32)) for i in range(4)]
        nsci = st.enter_context(nc.sbuf_tensor("nsci", [128, 8], I32))
        keps = st.enter_context(nc.sbuf_tensor("keps", [128, 1], F32))
        qmax = st.enter_context(nc.sbuf_tensor("qmax", [128, 8], F32))
        kqmax = st.enter_context(nc.sbuf_tensor("kqmax", [128, 8], F32))
        ctmp = [st.enter_context(nc.sbuf_tensor(f"ctmp{i}", [128, 8], F32)) for i in range(2)]
        negc8 = st.enter_context(nc.sbuf_tensor("negc8", [128, 8], BF16))

        ABF_COLS = 73792
        AF_COLS = 13568
        abf_t = st.enter_context(nc.sbuf_tensor("abf", [128, ABF_COLS], BF16))
        af_t = st.enter_context(nc.sbuf_tensor("af", [128, AF_COLS], F32))
        A = Arena(abf_t, "bf", ABF_COLS, blk=512)
        Z = Arena(af_t, "f3", AF_COLS, blk=1024)

        ps01 = st.enter_context(nc.psum_tensor("ps01", [128, 1024], F32))
        ps45 = st.enter_context(nc.psum_tensor("ps45", [128, 1024], F32))
        PS = [ps01[:, 0:512], ps01[:, 512:1024]] + [
            st.enter_context(nc.psum_tensor(f"ps{i}", [128, 512], F32)) for i in range(2, 4)] + [
            ps45[:, 0:512], ps45[:, 512:1024]]
        PT = st.enter_context(nc.psum_tensor("pt", [128, 1024], BF16))
        PT2 = st.enter_context(nc.psum_tensor("pt2", [128, 1024], BF16))
        PTS = [(PT, "P:T"), (PT2, "P:T2")]

        p = Prog(nc, st)

        p.op("Y", lambda y: y.dma_start(out=kident[:, :], in_=ident_in), writes=["kident"], dma="dc")
        p.op("Y", lambda y: y.dma_start(out=khead[:, :], in_=headones_in), writes=["khead"], dma="dc")
        p.op("Y", lambda y: y.dma_start(out=kmask[:, :], in_=bigmask_in), writes=["kmask"], dma="dc")
        p.op("Y", lambda y: y.dma_start(out=kmask01[:, :], in_=mask01_in), writes=["kmask01"], dma="dc")
        p.op("V", lambda v: v.memset(kones[:, :], 1.0), writes=["kones"])
        p.op("V", lambda v: v.memset(krsi[:, :], 1.0), writes=["krsi"])
        p.op("V", lambda v: v.memset(keps[:, :], EPS), writes=["keps"])

        A.place("R2", 0, 32768)
        A.place("rowp", 32768, 8192)
        A.place("wtb", 73728, 64)
        A.place("tbuf", 40960, 4096)
        A.place("coefrep", 45056, 4096)
        A.place("rawk", 49152, 8192)
        A.place("ttmp0", 57344, 2048)
        A.place("ttmp1", 59392, 2048)
        A.place("stag", 61440, 8192)
        A.place("wes", 40960, 8192)
        A.place("xn", 49152, 4096)
        A.place("xnT", 53248, 4096)
        A.place("xnT2", 36864, 4096)
        A.place("hh", 57344, 16384)
        A.place("wv", 0, 8192)
        A.place("wos", 8192, 8192)
        A.place("wqk", 16384, 4096)
        A.place("axn", 20480, 2048)
        A.place("axnT", 22528, 4096)
        A.place("qs", 26624, 4096)
        A.place("pb2", 30720, 2048)
        A.place("nsq", 30720, 2048)
        A.place("os", 32768, 4096)
        A.place("ks", 36864, 16384)
        A.place("vs", 53248, 16640)
        A.place("axn2", 70144, 2048)
        A.place("pb3", 72192, 1024)

        Z.place("xb", 0, 8192)
        Z.place("yb", 8192, 2048)
        Z.place("fscr", 0, 4096)
        Z.place("fident", 10240, 128)
        Z.place("hcoefU", 10368, 320)
        Z.place("hcoefD", 10688, 320)
        Z.place("xa", 0, 8192)
        Z.place("x2b", 8192, 2048)
        Z.place("srow", 10240, 512)
        Z.place("srep", 10752, 512)
        Z.place("kbrowF", 11264, 2048)
        Z.place("srowRF", 13312, 256)

        rowp = lambda pp, c0, cn: A.view("rowp", pp, 1, c0, cn)
        kbrow = lambda c0, cn: Z.view("kbrowF", 0, 1, 0, 2048).bitcast(BF16)[0:1, c0:c0 + cn]
        kbt = Z.tags("kbrowF")
        PBS = [("pb2", 0), ("pb2", 1024), ("pb3", 0)]
        srow_v = lambda: Z.view("srow", 0, 1, 0, 512)
        srowR_v = lambda: Z.view("srowRF", 0, 1, 0, 256).bitcast(BF16)
        srep_v = lambda: Z.view("srep", 0, 64, 0, 512)

        def load_hcoef(w_dram, name):
            for si, s in enumerate(range(7, 12)):
                p.op("S", lambda y, s=s, si=si, w_dram=w_dram, name=name: y.dma_start(
                    out=Z.view(name, 0, 128, si * 64, 64).rearrange("p (q ab) -> p q ab", ab=4),
                    in_=w_dram[s].rearrange("(q p) a b -> p q (a b)", p=128)),
                    writes=Z.tags(name, si * 64, 64), dma="dhc")

        load_hcoef(up_w_in, "hcoefU")
        load_hcoef(dn_w_in, "hcoefD")

        for src, dst, nm in ((up_w_in, up_bf, "ubf"), (dn_w_in, dn_bf, "dbf")):
            p.op("G", lambda g, src=src, dst=dst: g.dma_start(
                out=dst.rearrange("s k a b -> (s k a b)").unsqueeze(0),
                in_=src.rearrange("s k a b -> (s k a b)").unsqueeze(0)),
                writes=[f"dram_{nm}"], dma="dpc")

        def pack_qk():
            for pr in range(8):
                for hh in range(2):
                    for src, dst, nm in ((wq_in, wqd, "wq"), (wk_in, wkd, "wk")):
                        p.op("G", lambda g, src=src, dst=dst, pr=pr, hh=hh: g.dma_start(
                            out=dst[pr][:, :, 64 * hh:64 * hh + 64],
                            in_=src[2 * pr + hh].rearrange("(c p) d -> p c d", p=128)),
                            writes=[f"dram_{nm}:{pr}:{hh}"], dma=f"dpk{pr % 2}")

        def build_T(w_dram, wtag, transposed, reverse):
            p.op("Y", lambda y: y.dma_start(out=A.view("tbuf"), in_=tident_in),
                 writes=A.tags("tbuf"), dma="dc")
            order = list(range(7))
            if reverse:
                order = order[::-1]
            i01, i10 = (2, 1) if transposed else (1, 2)
            for st_i in order:
                s = 1 << st_i
                snm = "rowp" if st_i % 2 == 0 else "stag"
                for lh in range(2):
                    p.op("Y", lambda y, st_i=st_i, lh=lh, snm=snm, w_dram=w_dram: y.dma_start(
                        out=A.view(snm, 0, 1, lh * 4096, 4096),
                        in_=w_dram[st_i].rearrange("k a b -> (k a b)").unsqueeze(0)[0:1, lh * 4096:(lh + 1) * 4096]),
                        reads=[wtag],
                        writes=A.tags(snm, lh * 4096, 4096), dma="dcf")
                for half in range(2):
                    for b in range(8):
                        pv, ptag = PS[b % 6], f"P:{b % 6}"
                        p.op("T", lambda t, half=half, b=b, snm=snm, pv=pv: t.matmul(
                            pv[:, :], kones[0:1, 0:128],
                            A.view(snm, 0, 1, half * 4096 + b * 512, 512), start=True, stop=True),
                            reads=A.tags(snm, half * 4096 + b * 512, 512) + ["kones"], writes=[ptag])
                        p.op("S", lambda s_, b=b, pv=pv: s_.copy(
                            A.view("coefrep", 0, 128, b * 512, 512), pv[:, :]),
                            reads=[ptag], writes=A.tags("coefrep", b * 512, 512))
                    f0 = half * 2048
                    tags_t = A.tags("tbuf", f0, 2048)
                    tags_c = A.tags("coefrep")

                    def A_v(st_i=st_i, s=s, f0=f0):
                        tv = A.view("tbuf", 0, 128, f0, 2048)
                        tv4 = tv.rearrange("p (nb two s) -> p nb two s", two=2, s=s)
                        return tv4[:, :, 0, :]

                    def B_v(st_i=st_i, s=s, f0=f0):
                        tv = A.view("tbuf", 0, 128, f0, 2048)
                        tv4 = tv.rearrange("p (nb two s) -> p nb two s", two=2, s=s)
                        return tv4[:, :, 1, :]

                    def W_v(j, s=s):
                        cr = A.view("coefrep")
                        cr4 = cr.rearrange("p (nb s four) -> p nb s four", s=s, four=4)
                        return cr4[:, :, :, j]

                    def T_v(name, s=s):
                        return A.view(name, 0, 128, 0, 1024).rearrange("p (nb s) -> p nb s", s=s)

                    p.op("V", lambda v, B_v=B_v, W_v=W_v, T_v=T_v, i01=i01: v.tensor_tensor(
                        T_v("ttmp0"), B_v(), W_v(i01), ALU.mult),
                        reads=tags_t + tags_c, writes=A.tags("ttmp0", 0, 1024))
                    p.op("V", lambda v, B_v=B_v, W_v=W_v, T_v=T_v: v.tensor_tensor(
                        T_v("ttmp1"), B_v(), W_v(3), ALU.mult),
                        reads=tags_t + tags_c, writes=A.tags("ttmp1", 0, 1024))
                    p.op("V", lambda v, A_v=A_v, B_v=B_v, W_v=W_v, i10=i10: v.tensor_tensor(
                        B_v(), A_v(), W_v(i10), ALU.mult),
                        reads=tags_t + tags_c, writes=tags_t)
                    p.op("V", lambda v, B_v=B_v, T_v=T_v: v.tensor_tensor(
                        B_v(), B_v(), T_v("ttmp1"), ALU.add),
                        reads=tags_t + A.tags("ttmp1", 0, 1024), writes=tags_t)
                    p.op("V", lambda v, A_v=A_v, W_v=W_v: v.tensor_tensor(
                        A_v(), A_v(), W_v(0), ALU.mult),
                        reads=tags_t + tags_c, writes=tags_t)
                    p.op("V", lambda v, A_v=A_v, T_v=T_v: v.tensor_tensor(
                        A_v(), A_v(), T_v("ttmp0"), ALU.add),
                        reads=tags_t + A.tags("ttmp0", 0, 1024), writes=tags_t)

        def high_stages(tags_fn, view_fn, coef, transposed, reverse, width):
            order = list(range(5))
            if reverse:
                order = order[::-1]
            i01, i10 = (2, 1) if transposed else (1, 2)
            for si in order:
                stride = 1 << (7 + si)
                cstep = stride // 128
                pi = 0
                for c in range(32):
                    if c & cstep:
                        continue
                    c2 = c + cstep
                    col = (si * 16 + pi) * 4
                    pi += 1
                    w = lambda j, col=col, coef=coef: Z.view(coef, 0, 128, col + j, 1)
                    zt = Z.tags(coef, col, 4)
                    ta, tb_ = tags_fn(c), tags_fn(c2)
                    eng = "V"
                    tt0 = A.tags("ttmp0", 0, width) if width > 1 else A.tags("ttmp0", 0, 1)
                    tt1 = A.tags("ttmp1", 0, width) if width > 1 else A.tags("ttmp1", 0, 1)
                    p.op(eng, lambda v, c2=c2, w=w, i01=i01, width=width: v.tensor_scalar(
                        A.view("ttmp0", 0, 128, 0, width), view_fn(c2), w(i01), None, ALU.mult),
                        reads=tb_ + zt, writes=tt0)
                    p.op(eng, lambda v, c2=c2, w=w, width=width: v.tensor_scalar(
                        A.view("ttmp1", 0, 128, 0, width), view_fn(c2), w(3), None, ALU.mult),
                        reads=tb_ + zt, writes=tt1)
                    p.op(eng, lambda v, c=c, c2=c2, w=w, i10=i10, width=width: v.scalar_tensor_tensor(
                        view_fn(c2), view_fn(c), w(i10),
                        A.view("ttmp1", 0, 128, 0, width), ALU.mult, ALU.add),
                        reads=ta + tt1 + zt, writes=tb_)
                    p.op(eng, lambda v, c=c, w=w, width=width: v.scalar_tensor_tensor(
                        view_fn(c), view_fn(c), w(0),
                        A.view("ttmp0", 0, 128, 0, width), ALU.mult, ALU.add),
                        reads=ta + tt0 + zt, writes=ta)

        build_T(up_bf, "dram_ubf", transposed=False, reverse=False)

        p.op("G", lambda g: g.dma_start(out=rowp(32, 0, 4096), in_=b_exp_in.unsqueeze(0)),
             writes=A.tags("rowp"), dma="dcf")
        for cg in range(4):
            p.op("T", [
                (lambda t, c=8 * cg + j: t.transpose(
                    PT[:, 2 * c:2 * c + 1], rowp(32, 128 * c, 128), kident[32:33, 32:33]))
                for j in range(8)],
                reads=A.tags("rowp") + ["kident"], writes=["P:T"])
        p.op("V", lambda v: v.tensor_copy(
            A.view("wtb", 0, 128, 0, 32),
            PT[:, 0:64].rearrange("p (a b) -> p a b", b=2)[:, :, 0]),
            reads=["P:T"], writes=A.tags("wtb"))

        for k in range(8):
            kb = k % 2
            p.op("G", lambda g, k=k, kb=kb: g.dma_start(
                out=A.view("rawk", 0, 128, kb * 4096, 4096),
                in_=w_exp_in.rearrange("(c p) f -> c p f", p=128)[k]),
                writes=A.tags("rawk", kb * 4096, 4096), dma=f"dre{kb}")
            for cg in range(4):
                ptile, pttag = PTS[cg % 2]
                p.op("T", [
                    (lambda t, kb=kb, c=8 * cg + j, j=j, ptile=ptile: t.transpose(
                        ptile[:, 128 * j:128 * (j + 1)],
                        A.view("rawk", 0, 128, kb * 4096 + 128 * c, 128),
                        kident[:, :]))
                    for j in range(8)],
                    reads=A.tags("rawk", kb * 4096, 4096) + ["kident"], writes=[pttag])
                p.op("V", lambda v, k=k, cg=cg, ptile=ptile: v.tensor_copy(
                    A.view("R2").rearrange("p (c q) -> p c q", q=1024)[:, 8 * cg:8 * cg + 8, 128 * k:128 * (k + 1)],
                    ptile[:, :].rearrange("p (j q) -> p j q", q=128)),
                    reads=[pttag],
                    writes=[t for j in range(8) for t in A.tags("R2", (8 * cg + j) * 1024 + 128 * k, 128)])

        for c in range(32):
            for half in range(2):
                pv = PS[2 + (c * 2 + half) % 4]
                ptag = f"P:{2 + (c * 2 + half) % 4}"
                p.op("T", lambda t, c=c, half=half, pv=pv: t.matmul(
                    pv[:, :], A.view("tbuf", 0, 128, 128 * c, 128),
                    A.view("R2", 0, 128, c * 1024 + half * 512, 512),
                    start=True, stop=True),
                    reads=A.tags("tbuf", 128 * c, 128) + A.tags("R2", c * 1024 + half * 512, 512),
                    writes=[ptag])
                p.op("S", lambda s_, c=c, half=half, pv=pv: s_.copy(
                    A.view("R2", 0, 128, c * 1024 + half * 512, 512), pv[:, :]),
                    reads=[ptag], writes=A.tags("R2", c * 1024 + half * 512, 512))
        p.op("T", [
            (lambda t, c=c: t.matmul(PS[4][:, c:c + 1],
                                     A.view("tbuf", 0, 128, 128 * c, 128),
                                     A.view("wtb", 0, 128, c, 1), start=True, stop=True))
            for c in range(32)],
            reads=A.tags("tbuf") + A.tags("wtb"), writes=["P:4"])
        p.op("V", lambda v: v.tensor_copy(A.view("wtb", 0, 128, 0, 32), PS[4][:, 0:32]),
             reads=["P:4"], writes=A.tags("wtb"))

        high_stages(lambda c: A.tags("R2", c * 1024, 1024),
                    lambda c: A.view("R2", 0, 128, c * 1024, 1024),
                    "hcoefU", transposed=False, reverse=False, width=1024)
        high_stages(lambda c: A.tags("wtb"),
                    lambda c: A.view("wtb", 0, 128, c, 1),
                    "hcoefU", transposed=False, reverse=False, width=1)

        for k in range(8):
            kb = k % 2
            for cg in range(4):
                ptile, pttag = PTS[cg % 2]
                p.op("T", [
                    (lambda t, k=k, c=8 * cg + j, j=j, ptile=ptile: t.transpose(
                        ptile[:, 128 * j:128 * (j + 1)],
                        A.view("R2", 0, 128, c * 1024 + 128 * k, 128),
                        kident[:, :]))
                    for j in range(8)],
                    reads=[tg for j in range(8) for tg in A.tags("R2", (8 * cg + j) * 1024 + 128 * k, 128)] + ["kident"],
                    writes=[pttag])
                p.op("V", lambda v, kb=kb, cg=cg, ptile=ptile: v.tensor_copy(
                    A.view("stag", 0, 128, kb * 4096 + cg * 1024, 1024), ptile[:, :]),
                    reads=[pttag], writes=A.tags("stag", kb * 4096 + cg * 1024, 1024))
            p.op("G", lambda g, k=k, kb=kb: g.dma_start(
                out=wexp_s[:, :, k, :].rearrange("m r f -> r m f"),
                in_=A.view("stag", 0, 128, kb * 4096, 4096).rearrange("r (m f) -> r m f", f=128)),
                reads=A.tags("stag", kb * 4096, 4096), writes=[f"dram_we:{k}"], dma=f"dse{kb}")

        p.op("G", lambda g: g.dma_start(out=Z.view("fscr", 0, 1, 0, 4096), in_=up_b_in.unsqueeze(0)),
             writes=Z.tags("fscr"), dma="dcf")
        p.op("V", lambda v: v.tensor_copy(Z.view("fident"), kident[:, :]),
             reads=["kident"], writes=Z.tags("fident"))
        for cg in range(4):
            p.op("T", [
                (lambda t, c=8 * cg + j, j=j: t.transpose(
                    PS[5][:, j:j + 1], Z.view("fscr", 0, 1, 128 * c, 128), Z.view("fident", 0, 1, 0, 1)))
                for j in range(8)],
                reads=Z.tags("fscr") + Z.tags("fident"), writes=["P:5"])
            p.op("V", lambda v, cg=cg: v.tensor_tensor(
                bias1[:, 8 * cg:8 * cg + 8], PS[5][:, 0:8],
                A.view("wtb", 0, 128, 8 * cg, 8), ALU.add),
                reads=["P:5"] + A.tags("wtb"), writes=["bias1"])

        for c in range(32):
            p.op("G", lambda g, c=c: g.dma_start(
                out=A.view("R2", 0, 128, c * 1024, 1024),
                in_=w_con_in.rearrange("(c p) d -> c p d", p=128)[c]),
                writes=A.tags("R2", c * 1024, 1024), dma=f"drc{c % 2}")
        high_stages(lambda c: A.tags("R2", c * 1024, 1024),
                    lambda c: A.view("R2", 0, 128, c * 1024, 1024),
                    "hcoefD", transposed=True, reverse=True, width=1024)
        build_T(dn_bf, "dram_dbf", transposed=True, reverse=True)
        for c in range(32):
            for half in range(2):
                pv = PS[2 + (c * 2 + half) % 4]
                ptag = f"P:{2 + (c * 2 + half) % 4}"
                p.op("T", lambda t, c=c, half=half, pv=pv: t.matmul(
                    pv[:, :], A.view("tbuf", 0, 128, 128 * c, 128),
                    A.view("R2", 0, 128, c * 1024 + half * 512, 512),
                    start=True, stop=True),
                    reads=A.tags("tbuf", 128 * c, 128) + A.tags("R2", c * 1024 + half * 512, 512),
                    writes=[ptag])
                p.op("S", lambda s_, c=c, half=half, pv=pv: s_.copy(
                    A.view("R2", 0, 128, c * 1024 + half * 512, 512), pv[:, :]),
                    reads=[ptag], writes=A.tags("R2", c * 1024 + half * 512, 512))
        for c in range(32):
            p.op("G", lambda g, c=c: g.dma_start(
                out=wcon_s[c], in_=A.view("R2", 0, 128, c * 1024, 1024)),
                reads=A.tags("R2", c * 1024, 1024), writes=[f"dram_wc:{c}"], dma=f"dsc{c % 2}")

        def bias2_setup():
            p.op("G", lambda g: g.dma_start(out=Z.view("fscr", 0, 1, 0, 1024), in_=b_con_in.unsqueeze(0)),
                 writes=Z.tags("fscr"), dma="dcf")
            p.op("G", lambda g: g.dma_start(out=Z.view("fscr", 0, 1, 1024, 1024), in_=dn_b_in.unsqueeze(0)),
                 writes=Z.tags("fscr"), dma="dcf")
            p.op("V", lambda v: v.tensor_tensor(
                rowp(0, 0, 1024), Z.view("fscr", 0, 1, 0, 1024),
                Z.view("fscr", 0, 1, 1024, 1024), ALU.add),
                reads=Z.tags("fscr"), writes=A.tags("rowp"))

        bias2_setup()

        def newton_rsqrt(scalar_ok=False):
            if scalar_ok:
                p.op("S", lambda s_: s_.activation(nsc[0][:, 0:4], stats[:, 4:8], AF.Ln,
                                                   bias=keps[:, 0:1], scale=1.0),
                     reads=["stats", "keps"], writes=["nsc0"])
                p.op("S", lambda s_: s_.activation(stats[:, 4:8], nsc[0][:, 0:4], AF.Exp, scale=-0.5),
                     reads=["nsc0"], writes=["stats"])
                return
            v4 = lambda t: t[:, 0:4]
            p.op("V", lambda v: v.tensor_scalar(v4(nsc[0]), stats[:, 4:8], EPS, None, ALU.add),
                 reads=["stats"], writes=["nsc0"])
            p.op("V", lambda v: v.tensor_single_scalar(
                v4(nsci), v4(nsc[0]).bitcast(I32), 1, ALU.logical_shift_right),
                reads=["nsc0"], writes=["nsci"])
            p.op("V", lambda v: v.tensor_scalar(
                v4(nsci), v4(nsci), -1, 0x5f3759df, ALU.mult, ALU.add),
                reads=["nsci"], writes=["nsci"])
            p.op("V", lambda v: v.tensor_copy(v4(nsc[1]), v4(nsci).bitcast(F32)),
                 reads=["nsci"], writes=["nsc1"])
            for _ in range(3):
                p.op("V", lambda v: v.tensor_tensor(v4(nsc[2]), v4(nsc[1]), v4(nsc[1]), ALU.mult),
                     reads=["nsc1"], writes=["nsc2"])
                p.op("V", lambda v: v.tensor_tensor(v4(nsc[2]), v4(nsc[2]), v4(nsc[0]), ALU.mult),
                     reads=["nsc2", "nsc0"], writes=["nsc2"])
                p.op("V", lambda v: v.tensor_scalar(v4(nsc[2]), v4(nsc[2]), -0.5, 1.5, ALU.mult, ALU.add),
                     reads=["nsc2"], writes=["nsc2"])
                p.op("V", lambda v: v.tensor_tensor(v4(nsc[1]), v4(nsc[1]), v4(nsc[2]), ALU.mult),
                     reads=["nsc1", "nsc2"], writes=["nsc1"])
            p.op("V", lambda v: v.tensor_copy(stats[:, 4:8], v4(nsc[1])),
                 reads=["nsc1"], writes=["stats"])

        def emit_ln_tr(xsrc_view, xsrc_tags, xslots, xnT_name, rsqrt_scalar=False):
            nslots = len(xslots)
            for t in range(4):
                p.op("V", lambda v, t=t: v.bn_stats(nsc[3][:, 0:6], xsrc_view(1024 * t, 512)),
                     reads=xsrc_tags(1024 * t, 512), writes=["nsc3"])
                p.op("V", lambda v, t=t: v.bn_stats(nsc[3][:, 6:12], xsrc_view(1024 * t + 512, 512)),
                     reads=xsrc_tags(1024 * t + 512, 512), writes=["nsc3"])
                p.op("V", lambda v, t=t: v.bn_aggr(
                    stats[:, :].rearrange("p (a b) -> p a b", b=4)[:, :, t], nsc[3][:, 0:12]),
                    reads=["nsc3"], writes=["stats"])
            newton_rsqrt(scalar_ok=rsqrt_scalar)

            def emit_norm(t):
                nm, sl = xslots[t % nslots]
                p.op("V", lambda v, t=t, nm=nm, sl=sl: v.tensor_scalar(
                    A.view(nm, 0, 128, sl, 1024), xsrc_view(1024 * t, 1024),
                    stats[:, t:t + 1], stats[:, 4 + t:5 + t], ALU.subtract, ALU.mult),
                    reads=xsrc_tags(1024 * t, 1024) + ["stats"],
                    writes=A.tags(nm, sl, 1024))

            def emit_tr(t):
                nm, sl = xslots[t % nslots]
                ptile, pttag = PTS[t % 2]
                p.op("T", [
                    (lambda tt, t=t, nm=nm, sl=sl, c=c, ptile=ptile: tt.transpose(
                        ptile[:, 128 * c:128 * (c + 1)],
                        A.view(nm, 0, 128, sl + 128 * c, 128), kident[:, :]))
                    for c in range(8)],
                    reads=A.tags(nm, sl, 1024) + ["kident"], writes=[pttag])
                p.op("V", lambda e, t=t, xnT_name=xnT_name, ptile=ptile: e.tensor_copy(
                    A.view(xnT_name).rearrange("p (c q) -> p c q", q=512)[:, :, 128 * t:128 * (t + 1)],
                    ptile[:, :].rearrange("p (c q) -> p c q", q=128)),
                    reads=[pttag],
                    writes=[tg for c in range(8) for tg in A.tags(xnT_name, 512 * c + 128 * t, 128)])

            if nslots >= 4:
                for t in range(4):
                    emit_norm(t)
                for t in range(4):
                    emit_tr(t)
            else:
                for t in range(4):
                    emit_norm(t)
                    emit_tr(t)

        def emit_ffn(src_d, dst_d, tag_pfx):
            def load_x(s):
                sb = s % 2
                p.op("Y", lambda y, s=s, sb=sb: y.dma_start(
                    out=Z.view("xb", 0, 128, sb * 4096, 4096).rearrange("p (t f) -> p t f", f=1024),
                    in_=src_d[SUP * s:SUP * (s + 1)].rearrange("(t p) f -> p t f", p=128)),
                    writes=Z.tags("xb", sb * 4096, 4096), dma=f"dx{sb}")

            def ln_super(s):
                sb = s % 2
                xv_ = lambda c0, cn, sb=sb: Z.view("xb", 0, 128, sb * 4096 + c0, cn)
                xt_ = lambda c0, cn, sb=sb: Z.tags("xb", sb * 4096 + c0, cn)
                emit_ln_tr(xv_, xt_, [("xn", i * 1024) for i in range(4)],
                           "xnT" if s % 2 == 0 else "xnT2")
                return xv_, xt_

            load_x(0)
            xv, xt = ln_super(0)
            for s in range(NSUP):
                sb = s % 2
                xnT_nm = "xnT" if s % 2 == 0 else "xnT2"
                if s + 1 < NSUP:
                    load_x(s + 1)

                def wes_load(qd):
                    sl = (4 * qd) % 8
                    p.op("G", lambda g, qd=qd, sl=sl: g.dma_start(
                        out=A.view("wes", 0, 128, sl * 1024, 4096).rearrange("r (m kf) -> r m kf", kf=1024),
                        in_=wexp_s[4 * qd:4 * qd + 4].rearrange("m r k f -> r m (k f)")),
                        reads=[f"dram_we:{k}" for k in range(8)],
                        writes=A.tags("wes", sl * 1024, 4096), dma=f"dwe{qd % 2}")

                wes_load(0)
                wes_load(1)

                for m in range(32):
                    pv, ptag = PS[m % 2], f"P:{m % 2}"
                    p.op("T", [
                        (lambda t, m=m, k=k, pv=pv, xnT_nm=xnT_nm: t.matmul(
                            pv[:, :],
                            A.view("wes", 0, 128, (m % 8) * 1024 + 128 * k, 128),
                            A.view(xnT_nm, 0, 128, 512 * k, 512),
                            start=(k == 0), stop=(k == 7)))
                        for k in range(8)],
                        reads=A.tags("wes", (m % 8) * 1024, 1024) + A.tags(xnT_nm),
                        writes=[ptag])
                    if m % 4 == 3 and (m + 5) // 4 < 8:
                        wes_load((m + 5) // 4)
                    p.op("S", lambda s_, m=m, pv=pv: s_.activation(
                        A.view("hh", 0, 128, 512 * m, 512), pv[:, :], AF.Gelu,
                        bias=bias1[:, m:m + 1], scale=1.0),
                        reads=[ptag, "bias1"], writes=A.tags("hh", 512 * m, 512))

                xv_cur, xt_cur = xv, xt
                if s + 1 < NSUP:
                    xv, xt = ln_super(s + 1)

                for t in range(4):
                    for half in range(2):
                        pv = PS[2 + (t * 2 + half) % 2]
                        ptag = f"P:{2 + (t * 2 + half) % 2}"
                        blds = [
                            (lambda tt, k=k, t=t, half=half, pv=pv: tt.matmul(
                                pv[:, :],
                                A.view("hh", 0, 128, 512 * k + 128 * t, 128),
                                A.view("R2", 0, 128, 1024 * k + 512 * half, 512),
                                start=(k == 0), stop=False))
                            for k in range(32)]
                        blds.append(lambda tt, half=half, pv=pv: tt.matmul(
                            pv[:, :], kones[0:1, 0:128], rowp(0, half * 512, 512),
                            start=False, stop=True))
                        p.op("T", blds,
                             reads=A.tags("hh") + A.tags("R2") + ["kones"] + A.tags("rowp"),
                             writes=[ptag])
                        yb = t % 2
                        p.op("V", lambda v, t=t, half=half, pv=pv, yb=yb, xv=xv_cur: v.scalar_tensor_tensor(
                            Z.view("yb", 0, 128, yb * 1024 + half * 512, 512),
                            pv[:, :], 0.5, xv(1024 * t + 512 * half, 512),
                            ALU.mult, ALU.add),
                            reads=[ptag] + xt_cur(1024 * t + 512 * half, 512),
                            writes=Z.tags("yb", yb * 1024 + half * 512, 512))
                    yb = t % 2
                    eng = "G" if t % 2 == 0 else "Y"
                    p.op(eng, lambda e, s=s, t=t, yb=yb: e.dma_start(
                        out=dst_d[SUP * s + 128 * t:SUP * s + 128 * (t + 1)],
                        in_=Z.view("yb", 0, 128, yb * 1024, 1024)),
                        reads=Z.tags("yb", yb * 1024, 1024), dma=f"dy{yb}")

        emit_ffn(x_in, x1_d, "f1")
        pack_qk()
        p.fence()

        for c in range(8):
            p.op("G", lambda g, c=c: g.dma_start(
                out=A.view("wv", 0, 128, 1024 * c, 1024).rearrange("p (h d) -> p h d", d=64),
                in_=wv_in.rearrange("h (c p) d -> c p h d", p=128)[c]),
                writes=A.tags("wv", 1024 * c, 1024), dma="dwv")
        for c in range(8):
            p.op("G", lambda g, c=c: g.dma_start(
                out=A.view("wos", 0, 128, 1024 * c, 1024),
                in_=wo_in.rearrange("(c p) d -> c p d", p=128)[c]),
                writes=A.tags("wos", 1024 * c, 1024), dma="dwo")
        p.op("G", lambda g: g.dma_start(out=kbrow(0, 1024),
                                        in_=bq_in.rearrange("h d -> (h d)").unsqueeze(0)),
             writes=kbt, dma="dcf")
        p.op("G", lambda g: g.dma_start(out=kbrow(1024, 1024),
                                        in_=bk_in.rearrange("h d -> (h d)").unsqueeze(0)),
             writes=kbt, dma="dcf")
        p.op("G", lambda g: g.dma_start(out=kbrow(2048, 1024),
                                        in_=bv_in.rearrange("h d -> (h d)").unsqueeze(0)),
             writes=kbt, dma="dcf")
        p.op("G", lambda g: g.dma_start(out=kbrow(3072, 1024), in_=bo_in.unsqueeze(0)),
             writes=kbt, dma="dcf")
        p.op("V", lambda v: v.memset(
            A.view("vs").rearrange("p (t h d) -> p t h d", t=16, d=65)[:, :, :, 64], 1.0),
            writes=A.tags("vs"))

        for q in range(NSEQ):
            base = q * SEQ
            xnT_v = lambda c0, cn: A.view("axnT", 0, 128, c0, cn)
            xnT_t = A.tags("axnT")
            PTf, PT2f = PT.bitcast(F32), PT2.bitcast(F32)

            def dxa_issue(u, base=base):
                xo = (u % 2) * 4096
                p.op("Y", lambda y, base=base, u=u, xo=xo: y.dma_start(
                    out=Z.view("xa", 0, 128, xo, 4096).rearrange("p (t f) -> p t f", f=1024),
                    in_=x1_d[base + SUP * u: base + SUP * (u + 1)].rearrange("(t p) f -> p t f", p=128)),
                    writes=Z.tags("xa", xo, 4096), dma=f"dxa{u % 2}")

            def ln_u(u):
                xo = (u % 2) * 4096
                xav = lambda c0, cn, xo=xo: Z.view("xa", 0, 128, xo + c0, cn)
                xat = lambda c0, cn, xo=xo: Z.tags("xa", xo + c0, cn)
                emit_ln_tr(xav, xat, [("axn", 0), ("axn", 1024), ("axn2", 0), ("axn2", 1024)],
                           "axnT", rsqrt_scalar=True)

            def wqk_load(pi):
                sl = (pi % 2) * 2048
                dram, nm = (wqd, "wq") if pi < 4 else (wkd, "wk")
                pr0 = (pi % 4) * 2
                p.op("G", lambda g, dram=dram, pr0=pr0, sl=sl: g.dma_start(
                    out=A.view("wqk", 0, 128, sl, 2048).rearrange("p (w cd) -> p w cd", cd=1024),
                    in_=dram[pr0:pr0 + 2].rearrange("w p c d -> p w (c d)")),
                    reads=[f"dram_{nm}:{pr}:{hh}" for pr in (pr0, pr0 + 1) for hh in (0, 1)],
                    writes=A.tags("wqk", sl, 2048), dma=f"dwqk{pi % 2}")

            def qk_group(pi, pp, uu):
                isk = pi >= 4
                bcol = 1024 if isk else 0
                pr = (pi % 4) * 2 + pp
                sl = (pi % 2) * 2048 + pp * 1024
                if isk:
                    pv, ptag = (PTf, "P:T") if pr % 2 == 0 else (PT2f, "P:T2")
                else:
                    pv, ptag = PS[pr % 2], f"P:{pr % 2}"
                blds = [
                    (lambda t, sl=sl, k=k, pv=pv: t.matmul(
                        pv[:, :],
                        A.view("wqk", 0, 128, sl + 128 * k, 128),
                        xnT_v(512 * k, 512), start=(k == 0), stop=False))
                    for k in range(8)]
                blds.append(lambda t, bcol=bcol, pr=pr, pv=pv: t.matmul(
                    pv[:, :], kbrow(bcol + 128 * pr, 128),
                    kones[0:1, 0:512], start=False, stop=True))
                p.op("T", blds,
                     reads=A.tags("wqk", sl, 1024) + xnT_t + kbt + ["kones"], writes=[ptag])
                if isk:
                    p.op("V", lambda v, pr=pr, uu=uu, pv=pv: v.tensor_copy(
                        A.view("ks", 0, 128, 2048 * pr + 512 * uu, 512), pv[:, :]),
                        reads=[ptag], writes=A.tags("ks", 2048 * pr + 512 * uu, 512))
                else:
                    p.op("S", lambda s_, pr=pr, pv=pv: s_.copy(
                        A.view("qs", 0, 128, 512 * pr, 512), pv[:, :]),
                        reads=[ptag], writes=A.tags("qs", 512 * pr, 512))

            def v_group(t, half, uu):
                pv, ptag = ((PTf, "P:T") if half == 0 else (PT2f, "P:T2"))
                blds = [
                    (lambda tt, t=t, half=half, k=k, pv=pv: tt.matmul(
                        pv[:, :], xnT_v(512 * k + 128 * t, 128),
                        A.view("wv", 0, 128, 1024 * k + 512 * half, 512),
                        start=(k == 0), stop=False))
                    for k in range(8)]
                blds.append(lambda tt, half=half, pv=pv: tt.matmul(
                    pv[:, :], kones[0:1, 0:128],
                    kbrow(2048 + half * 512, 512),
                    start=False, stop=True))
                p.op("T", blds,
                     reads=A.tags("wv") + xnT_t + kbt + ["kones"], writes=[ptag])
                p.op("V", lambda v, uu=uu, t=t, half=half, pv=pv: v.tensor_copy(
                    A.view("vs", 0, 128, 1040 * (4 * uu + t) + 520 * half, 520
                           ).rearrange("p (h d) -> p h d", d=65)[:, :, 0:64],
                    pv[:, :].rearrange("p (h d) -> p h d", d=64)),
                    reads=[ptag],
                    writes=A.tags("vs", 1040 * (4 * uu + t) + 520 * half, 520))

            def kv_items(uu):
                items = [lambda: wqk_load(4), lambda: wqk_load(5)]
                for pi in (4, 5, 6, 7):
                    for pp in (0, 1):
                        items.append(lambda pi=pi, pp=pp: qk_group(pi, pp, uu))
                    if pi + 2 < 8:
                        items.append(lambda pi=pi: wqk_load(pi + 2))
                for t in range(4):
                    for half in range(2):
                        items.append(lambda t=t, half=half: v_group(t, half, uu))
                return items

            dxa_issue(0)
            dxa_issue(1)
            ln_u(0)
            for u in range(4):
                xo = (u % 2) * 4096
                wqk_load(0)
                wqk_load(1)
                for pi in range(4):
                    for pp in range(2):
                        qk_group(pi, pp, u)
                    if pi + 2 < 4:
                        wqk_load(pi + 2)
                if u == 0:
                    for it in kv_items(0):
                        it()
                for mat, dstt, dsttag, fresh, off in (
                        ("qs", qmax, "qmax", True, lambda pr: 512 * pr),
                        ("ks", kqmax, "kqmax", u == 0, lambda pr, u=u: 2048 * pr + 512 * u)):
                    def nsq_sq(pr, mat=mat, off=off):
                        so = (pr % 4) * 512
                        p.op("V", lambda v, mat=mat, pr=pr, so=so, off=off: v.tensor_tensor(
                            A.view("nsq", 0, 128, so, 512), A.view(mat, 0, 128, off(pr), 512),
                            A.view(mat, 0, 128, off(pr), 512), ALU.mult),
                            reads=A.tags(mat, off(pr), 512), writes=A.tags("nsq", so, 512))

                    for pr in range(4):
                        nsq_sq(pr)
                    for pr in range(8):
                        so = (pr % 4) * 512
                        pv, ptag = PS[4 + pr % 2], f"P:{4 + pr % 2}"
                        p.op("T", lambda t, so=so, pv=pv: t.matmul(
                            pv[:, :], khead[:, :], A.view("nsq", 0, 128, so, 512),
                            start=True, stop=True),
                            reads=A.tags("nsq", so, 512) + ["khead"], writes=[ptag])
                        if pr + 4 < 8:
                            nsq_sq(pr + 4)
                        if fresh:
                            p.op("V", lambda v, dstt=dstt, pr=pr, pv=pv: v.tensor_reduce(
                                dstt[:, pr:pr + 1], pv[:, :], AX.X, ALU.max),
                                reads=[ptag], writes=[dsttag])
                        else:
                            p.op("V", lambda v, pv=pv: v.tensor_reduce(
                                ctmp[0][:, 0:1], pv[:, :], AX.X, ALU.max),
                                reads=[ptag], writes=["ctmp0"])
                            p.op("V", lambda v, dstt=dstt, pr=pr: v.tensor_tensor(
                                dstt[:, pr:pr + 1], dstt[:, pr:pr + 1], ctmp[0][:, 0:1], ALU.max),
                                reads=[dsttag, "ctmp0"], writes=[dsttag])
                p.op("S", lambda s_: s_.activation(ctmp[0][:, 0:8], qmax[:, 0:8], AF.Ln),
                     reads=["qmax"], writes=["ctmp0"])
                p.op("S", lambda s_: s_.activation(ctmp[1][:, 0:8], kqmax[:, 0:8], AF.Ln),
                     reads=["kqmax"], writes=["ctmp1"])
                p.op("V", lambda v: v.tensor_tensor(ctmp[0][:, 0:8], ctmp[0][:, 0:8], ctmp[1][:, 0:8], ALU.add),
                     reads=["ctmp0", "ctmp1"], writes=["ctmp0"])
                p.op("S", lambda s_: s_.activation(ctmp[1][:, 0:8], ctmp[0][:, 0:8], AF.Exp, scale=0.5),
                     reads=["ctmp0"], writes=["ctmp1"])
                p.op("V", lambda v: v.tensor_scalar(negc8[:, 0:8], ctmp[1][:, 0:8], -0.125, None, ALU.mult),
                     reads=["ctmp1"], writes=["negc8"])
                for hf in range(2):
                    pv, ptag = PS[4 + hf], f"P:{4 + hf}"
                    p.op("T", lambda t, hf=hf, pv=pv: t.matmul(
                        pv[:, 0:8], kones[64 * hf:64 * hf + 1, 0:128],
                        negc8[64 * hf:64 * hf + 1, 0:8], start=True, stop=True),
                        reads=["negc8", "kones"], writes=[ptag])
                    p.op("V", lambda v, hf=hf, pv=pv: v.tensor_copy(
                        negc.rearrange("p (a b) -> p a b", b=2)[:, :, hf], pv[:, 0:8]),
                        reads=[ptag], writes=["negc"])

                if u < 3:
                    ln_u(u + 1)
                    fill = kv_items(u + 1)
                else:
                    fill = []

                srow64_v = lambda: Z.view("srow", 64, 1, 0, 512)
                srowR64_v = lambda: Z.view("srowRF", 64, 1, 0, 256).bitcast(BF16)

                def tail_a(po, otag):
                    p.op("S", lambda s_, po=po: s_.activation(
                        srow64_v(), po[64:65, :], AF.Ln),
                        reads=[otag], writes=Z.tags("srow"))

                    def _rexp(s_):
                        with nc.allow_low_precision(reason="bf16 softmax recip (exp(-ln(sums))) validated in golden model"):
                            return s_.activation(srowR64_v(), srow64_v(), AF.Exp, scale=-1.0)
                    p.op("S", _rexp,
                         reads=Z.tags("srow"), writes=Z.tags("srowRF"))

                def tail_b(pr, hf, po, otag):
                    p.op("T", lambda t: t.matmul(
                        PS[4][0:64, :], kones[64:65, 0:64], srowR64_v(), start=True, stop=True),
                        reads=Z.tags("srowRF") + ["kones"], writes=["P:4"])
                    p.op("V", lambda v: v.tensor_copy(srep_v(), PS[4][0:64, :]),
                         reads=["P:4"], writes=Z.tags("srep"))
                    p.op("V", lambda v, pr=pr, hf=hf, po=po, otag=otag: v.tensor_tensor(
                        A.view("os", 64 * hf, 64, 512 * pr, 512), po[0:64, :],
                        srep_v(), ALU.mult),
                        reads=[otag] + Z.tags("srep"), writes=A.tags("os", 512 * pr, 512))

                pend_tail = None
                for h in range(NH):
                    pr, hf = h // 2, h % 2
                    kt = 4 * (u + 1)
                    po, otag = PS[2 + h % 2], f"P:{2 + h % 2}"

                    def pv_op(i, h=h, kt=kt, po=po, otag=otag):
                        pair = i // 2
                        pnm, poff = PBS[pair % 3]
                        pcol = poff + 512 * (i % 2)
                        p.op("T", lambda t, i=i, h=h, kt=kt, po=po, pnm=pnm, pcol=pcol: t.matmul(
                            po[0:65, :],
                            A.view("vs", 0, 128, 1040 * i + 65 * h, 65),
                            A.view(pnm, 0, 128, pcol, 512),
                            start=(i == 0), stop=(i == kt - 1)),
                            reads=A.tags("vs", 1040 * i + 65 * h, 65) + A.tags(pnm, pcol, 512),
                            writes=[otag], accum_tags=(otag,))

                    for pair in range(kt // 2):
                        bs = pair % 2
                        for ih in range(2):
                            i = 2 * pair + ih
                            pv, ptag = (PS[ih], f"P:{ih}") if bs == 0 else (PS[4 + ih], f"P:{4 + ih}")
                            p.op("T", lambda t, pr=pr, hf=hf, i=i, pv=pv: t.matmul(
                                pv[:, :],
                                A.view("ks", 64 * hf, 64, 2048 * pr + 128 * i, 128),
                                A.view("qs", 64 * hf, 64, 512 * pr, 512),
                                start=True, stop=True),
                                reads=A.tags("ks", 2048 * pr, 512 * (u + 1)) + A.tags("qs", 512 * pr, 512),
                                writes=[ptag])
                        pset = ps01 if bs == 0 else ps45
                        ptags = ["P:0", "P:1"] if bs == 0 else ["P:4", "P:5"]
                        pnm, poff = PBS[pair % 3]
                        p.op("S", lambda s_, h=h, pnm=pnm, poff=poff, pset=pset: s_.activation(
                            A.view(pnm, 0, 128, poff, 1024), pset[:, :], AF.Exp,
                            bias=negc[:, h:h + 1], scale=0.125),
                            reads=ptags + ["negc"],
                            writes=A.tags(pnm, poff, 1024))
                        for ih in range(2):
                            i = 2 * pair + ih
                            if i >= 4 * u:
                                d = i - 4 * u
                                p.op("V", lambda v, pnm=pnm, poff=poff, ih=ih, d=d: v.tensor_tensor(
                                    A.view(pnm, 0, 128, poff + 512 * ih, 512),
                                    A.view(pnm, 0, 128, poff + 512 * ih, 512),
                                    kmask01[:, 512 - 128 * d:1024 - 128 * d], ALU.mult),
                                    reads=A.tags(pnm, poff + 512 * ih, 512) + ["kmask01"],
                                    writes=A.tags(pnm, poff + 512 * ih, 512))
                        if pair == 0 and pend_tail is not None:
                            tail_b(*pend_tail)
                            pend_tail = None
                        if pair >= 2:
                            pv_op(2 * pair - 4)
                            pv_op(2 * pair - 3)
                    if fill:
                        take = (len(fill) + NH - 1 - h) // (NH - h)
                        for _ in range(take):
                            fill.pop(0)()
                    for i in range(max(kt - 4, 0), kt):
                        pv_op(i)
                    tail_a(po, otag)
                    pend_tail = (pr, hf, po, otag)
                while fill:
                    fill.pop(0)()
                tail_b(*pend_tail)
                for half in range(2):
                    for c in range(8):
                        for t in range(4):
                            pv, ptag = PS[2 + t], f"P:{2 + t}"
                            p.op("T", lambda tt, c=c, t=t, half=half, pv=pv: tt.matmul(
                                pv[:, :],
                                A.view("os", 0, 128, 512 * c + 128 * t, 128),
                                A.view("wos", 0, 128, 1024 * c + 512 * half, 512),
                                start=(c == 0), stop=False),
                                reads=A.tags("os", 512 * c, 512) + A.tags("wos", 1024 * c + 512 * half, 512),
                                writes=[ptag], accum_tags=(ptag,))
                    for t in range(4):
                        pv, ptag = PS[2 + t], f"P:{2 + t}"
                        p.op("T", lambda tt, half=half, pv=pv: tt.matmul(
                            pv[:, :], kones[0:1, 0:128],
                            kbrow(3072 + half * 512, 512),
                            start=False, stop=True),
                            reads=["kones"] + kbt, writes=[ptag], accum_tags=(ptag,))
                        p.op("V", lambda v, t=t, half=half, pv=pv, xo=xo: v.tensor_tensor(
                            Z.view("x2b", 0, 128, t * 512, 512), pv[:, :],
                            Z.view("xa", 0, 128, xo + t * 1024 + 512 * half, 512), ALU.add),
                            reads=[ptag] + Z.tags("xa", xo + t * 1024 + 512 * half, 512),
                            writes=Z.tags("x2b", t * 512, 512))
                        p.op("Y", lambda y, base=base, u=u, t=t, half=half: y.dma_start(
                            out=x2_d[base + 512 * u + 128 * t: base + 512 * u + 128 * (t + 1),
                                     512 * half:512 * (half + 1)],
                            in_=Z.view("x2b", 0, 128, t * 512, 512)),
                            reads=Z.tags("x2b", t * 512, 512), dma=f"dx2{t % 2}")
                if u + 2 <= 3:
                    dxa_issue(u + 2)
        p.fence()

        for c in range(32):
            p.op("G", lambda g, c=c: g.dma_start(
                out=A.view("R2", 0, 128, c * 1024, 1024), in_=wcon_s[c]),
                reads=[f"dram_wc:{c}"],
                writes=A.tags("R2", c * 1024, 1024), dma=f"drc{c % 2}")
        bias2_setup()
        emit_ffn(x2_d, y_out, "f2")
        p.fence()

        p.emit()
    return nc


_NC_CACHE = {}


def _get_nc():
    if "nc" not in _NC_CACHE:
        _NC_CACHE["nc"] = build_nc()
    return _NC_CACHE["nc"]


def _consts():
    ident = np.eye(128, dtype=BFD)
    tident = np.tile(np.eye(128, dtype=np.float32), (1, 32)).astype(BFD)
    pp = np.arange(128)[:, None]
    xx = np.arange(1024)[None, :]
    bigmask = np.where(pp <= xx - 512, 0.0, -30000.0).astype(BFD)
    mask01 = np.where(pp <= xx - 512, 1.0, 0.0).astype(BFD)
    ho = (np.arange(128)[:, None] // 64 == np.arange(128)[None, :] // 64)
    headones = ho.astype(BFD)
    return dict(cident=ident, ctident=tident, cbigmask=bigmask, cmask01=mask01,
                cheadones=headones)


def kernel(**inputs):
    x = np.ascontiguousarray(np.asarray(inputs["L0_attn_x"], np.float32))
    shards = x.reshape(NCORE, TOK, D)
    consts = _consts()
    weights = {nm: np.ascontiguousarray(np.asarray(inputs[nm], np.float32)) for nm in WNAMES}
    nc = _get_nc()
    in_maps = []
    for i in range(NCORE):
        m = {"x": shards[i]}
        m.update(weights)
        m.update(consts)
        in_maps.append(m)
    res = run_bass_kernel_spmd(nc, in_maps, core_ids=list(range(NCORE)))
    out = np.concatenate([np.asarray(r["y"], np.float32).reshape(NSEQ, SEQ, D)
                          for r in res.results], axis=0)
    return out.reshape(32, SEQ, D)


# version 22 (final)
# speedup vs baseline: 1.0431x; 1.0148x over previous
import contextlib
from collections import defaultdict

import numpy as np
import ml_dtypes

import concourse.bass as bass
import concourse.mybir as mybir
from concourse.bass_utils import run_bass_kernel_spmd

F32 = mybir.dt.float32
BF16 = mybir.dt.bfloat16
I32 = mybir.dt.int32
AF = mybir.ActivationFunctionType
ALU = mybir.AluOpType
AX = mybir.AxisListType
BFD = ml_dtypes.bfloat16

NCORE = 8
SEQ = 2048
NSEQ = 4
D = 1024
HDIM = 4096
NH = 16
TOK = NSEQ * SEQ
SUP = 512
NSUP = TOK // SUP
EPS = 1e-5
RS = float(2 ** 24)
RSI = float(2 ** -24)

WNAMES = [
    "L0_attn_Wq", "L0_attn_bq", "L0_attn_Wk", "L0_attn_bk", "L0_attn_Wv",
    "L0_attn_bv", "L0_attn_Wo", "L0_attn_bo", "L0_ffn_W_exp", "L0_ffn_b_exp",
    "L0_ffn_up_weights", "L0_ffn_up_bias", "L0_ffn_down_weights",
    "L0_ffn_W_con", "L0_ffn_b_con", "L0_ffn_down_bias",
]


class Prog:
    ENGINES = ("T", "V", "S", "G", "Y")

    def __init__(self, nc, stack):
        self.nc = nc
        self.stack = stack
        self.ops = []
        self.tag_w = {}
        self.tag_r = defaultdict(list)
        self.clock = {e: {} for e in self.ENGINES}
        self.op_clock = []
        self.dma_count = defaultdict(int)
        self.dma_clock = {}
        self.sems = {}

    def _sem(self, name):
        if name not in self.sems:
            self.sems[name] = self.stack.enter_context(self.nc.semaphore(name))
        return self.sems[name]

    @staticmethod
    def _join(dst, src):
        for k, v in src.items():
            if dst.get(k, -1) < v:
                dst[k] = v

    def _covered(self, clk, m):
        if m[0] == "e":
            return clk.get(("e", m[1]), -1) >= m[2]
        return clk.get(("d", m[1]), -1) >= m[2]

    def op(self, engine, builders, reads=(), writes=(), dma=None, accum_tags=()):
        if callable(builders):
            builders = [builders]
        idx = len(self.ops)
        e = engine
        waits = []

        def need(m):
            if m is None:
                return
            if m[0] == "d":
                m = ("d", m[1], self.dma_count[m[1]] * 16)
            if not self._covered(self.clock[e], m):
                if m not in waits:
                    waits.append(m)

        reads = list(reads)
        writes = list(writes)
        for t in reads:
            w = self.tag_w.get(t)
            if w is not None:
                if w[0] == "e" and w[1] == e and t in accum_tags:
                    pass
                else:
                    need(w)
        for t in writes:
            w = self.tag_w.get(t)
            if w is not None and not (w[0] == "e" and w[1] == e and t in accum_tags):
                need(w)
            for r in self.tag_r[t]:
                if r[0] == "e" and r[1] == e:
                    continue
                need(r)
        for t in reads:
            if t.startswith("P:") and t not in writes:
                for r in self.tag_r[t]:
                    if r[0] == "e" and r[1] == e:
                        continue
                    need(r)

        if dma is not None:
            self.dma_count[dma] += 1
            marker = ("d", dma, self.dma_count[dma] * 16)
        else:
            marker = ("e", e, idx)

        myclk = self.clock[e]
        for m in waits:
            if m[0] == "e":
                self._join(myclk, self.op_clock[m[2]])
                myclk[("e", m[1])] = max(myclk.get(("e", m[1]), -1), m[2])
            else:
                self._join(myclk, self.dma_clock[(m[1], m[2])])
                myclk[("d", m[1])] = max(myclk.get(("d", m[1]), -1), m[2])

        snap = dict(myclk)
        if dma is not None:
            snap[("d", dma)] = self.dma_count[dma] * 16
            self.dma_clock[(dma, self.dma_count[dma] * 16)] = snap
        else:
            snap[("e", e)] = idx
        self.op_clock.append(snap)

        for t in writes:
            self.tag_w[t] = marker
            self.tag_r[t] = []
        for t in reads:
            if t not in writes:
                self.tag_r[t].append(marker)

        self.ops.append(dict(engine=e, builders=builders, waits=waits, dma=dma,
                             idx=idx, inc_needed=False))
        return idx

    def fence(self):
        markers = []
        last_eng = {}
        for o in self.ops:
            if o["dma"] is None and o["builders"]:
                last_eng[o["engine"]] = o["idx"]
        for e, i in last_eng.items():
            markers.append(("e", e, i))
        for dsem, cnt in self.dma_count.items():
            markers.append(("d", dsem, cnt * 16))
        for e in self.ENGINES:
            idx = len(self.ops)
            waits = [m for m in markers
                     if not (m[0] == "e" and m[1] == e)
                     and not self._covered(self.clock[e], m)]
            myclk = self.clock[e]
            for m in waits:
                if m[0] == "e":
                    self._join(myclk, self.op_clock[m[2]])
                    myclk[("e", m[1])] = max(myclk.get(("e", m[1]), -1), m[2])
                else:
                    self._join(myclk, self.dma_clock[(m[1], m[2])])
                    myclk[("d", m[1])] = max(myclk.get(("d", m[1]), -1), m[2])
            snap = dict(myclk)
            snap[("e", e)] = idx
            self.op_clock.append(snap)
            self.ops.append(dict(engine=e, builders=[], waits=waits, dma=None,
                                 idx=idx, inc_needed=False))

    def emit(self):
        for o in self.ops:
            for m in o["waits"]:
                if m[0] == "e":
                    self.ops[m[2]]["inc_needed"] = True
        eng_rank = {}
        counters = defaultdict(int)
        for o in self.ops:
            if o["dma"] is None and o["inc_needed"]:
                counters[o["engine"]] += 1
                eng_rank[o["idx"]] = counters[o["engine"]]
        for e in self.ENGINES:
            if counters[e]:
                self._sem(f"sem_{e}")
        for dsem in list(self.dma_count):
            self._sem(dsem)

        per_engine = {e: [] for e in self.ENGINES}
        for o in self.ops:
            per_engine[o["engine"]].append(o)
        block = self.stack.enter_context(self.nc.Block())

        def run_stream(eng_obj, ops):
            for o in ops:
                for m in o["waits"]:
                    if m[0] == "e":
                        eng_obj.wait_ge(self._sem(f"sem_{m[1]}"), eng_rank[m[2]])
                    else:
                        eng_obj.wait_ge(self._sem(m[1]), m[2])
                n = len(o["builders"])
                for bi, b in enumerate(o["builders"]):
                    ins = b(eng_obj)
                    if bi == n - 1:
                        if o["dma"] is not None:
                            ins.then_inc(self._sem(o["dma"]), 16)
                        elif o["inc_needed"]:
                            ins.then_inc(self._sem(f"sem_{o['engine']}"), 1)

        section = {"T": block.tensor, "V": block.vector, "S": block.scalar,
                   "G": block.gpsimd, "Y": block.sync}
        for e in self.ENGINES:
            if per_engine[e]:
                def make(ops=per_engine[e]):
                    return lambda eng_obj: run_stream(eng_obj, ops)
                section[e](make())
        return block


class Arena:
    def __init__(self, tensor, name, cols, blk=2048):
        self.t = tensor
        self.name = name
        self.cols = cols
        self.blk = blk
        self.m = {}

    def place(self, name, off, n):
        assert off + n <= self.cols, f"{self.name}:{name} overflow {off}+{n}>{self.cols}"
        self.m[name] = (off, n)

    def view(self, name, p0=0, pn=128, c0=0, cn=None):
        off, n = self.m[name]
        if cn is None:
            cn = n - c0
        assert c0 + cn <= n, f"view {name} {c0}+{cn}>{n}"
        return self.t[p0:p0 + pn, off + c0: off + c0 + cn]

    def tags(self, name, c0=0, cn=None):
        off, n = self.m[name]
        if cn is None:
            cn = n - c0
        lo = (off + c0) // self.blk
        hi = (off + c0 + cn - 1) // self.blk
        return [f"{self.name}:{i}" for i in range(lo, hi + 1)]


def build_nc():
    nc = bass.Bass("TRN2", target_bir_lowering=False)

    x_in = nc.dram_tensor("x", [TOK, D], F32, kind="ExternalInput").ap()
    y_out = nc.dram_tensor("y", [TOK, D], F32, kind="ExternalOutput").ap()
    w_exp_in = nc.dram_tensor("L0_ffn_W_exp", [D, HDIM], F32, kind="ExternalInput").ap()
    b_exp_in = nc.dram_tensor("L0_ffn_b_exp", [HDIM], F32, kind="ExternalInput").ap()
    up_w_in = nc.dram_tensor("L0_ffn_up_weights", [12, 2048, 2, 2], F32, kind="ExternalInput").ap()
    up_b_in = nc.dram_tensor("L0_ffn_up_bias", [HDIM], F32, kind="ExternalInput").ap()
    dn_w_in = nc.dram_tensor("L0_ffn_down_weights", [12, 2048, 2, 2], F32, kind="ExternalInput").ap()
    w_con_in = nc.dram_tensor("L0_ffn_W_con", [HDIM, D], F32, kind="ExternalInput").ap()
    b_con_in = nc.dram_tensor("L0_ffn_b_con", [D], F32, kind="ExternalInput").ap()
    dn_b_in = nc.dram_tensor("L0_ffn_down_bias", [D], F32, kind="ExternalInput").ap()
    wq_in = nc.dram_tensor("L0_attn_Wq", [NH, D, 64], F32, kind="ExternalInput").ap()
    bq_in = nc.dram_tensor("L0_attn_bq", [NH, 64], F32, kind="ExternalInput").ap()
    wk_in = nc.dram_tensor("L0_attn_Wk", [NH, D, 64], F32, kind="ExternalInput").ap()
    bk_in = nc.dram_tensor("L0_attn_bk", [NH, 64], F32, kind="ExternalInput").ap()
    wv_in = nc.dram_tensor("L0_attn_Wv", [NH, D, 64], F32, kind="ExternalInput").ap()
    bv_in = nc.dram_tensor("L0_attn_bv", [NH, 64], F32, kind="ExternalInput").ap()
    wo_in = nc.dram_tensor("L0_attn_Wo", [D, D], F32, kind="ExternalInput").ap()
    bo_in = nc.dram_tensor("L0_attn_bo", [D], F32, kind="ExternalInput").ap()
    ident_in = nc.dram_tensor("cident", [128, 128], BF16, kind="ExternalInput").ap()
    tident_in = nc.dram_tensor("ctident", [128, HDIM], BF16, kind="ExternalInput").ap()
    bigmask_in = nc.dram_tensor("cbigmask", [128, 1024], BF16, kind="ExternalInput").ap()
    mask01_in = nc.dram_tensor("cmask01", [128, 1024], BF16, kind="ExternalInput").ap()
    headones_in = nc.dram_tensor("cheadones", [128, 128], BF16, kind="ExternalInput").ap()

    x1_d = nc.dram_tensor("x1", [TOK, D], F32, kind="Internal").ap()
    x2_d = nc.dram_tensor("x2", [TOK, D], F32, kind="Internal").ap()
    wexp_s = nc.dram_tensor("wexp_s", [32, 128, 8, 128], BF16, kind="Internal").ap()
    wcon_s = nc.dram_tensor("wcon_s", [32, 128, 1024], BF16, kind="Internal").ap()
    wqd = nc.dram_tensor("wqd", [8, 128, 8, 128], BF16, kind="Internal").ap()
    wkd = nc.dram_tensor("wkd", [8, 128, 8, 128], BF16, kind="Internal").ap()
    up_bf = nc.dram_tensor("up_bf", [12, 2048, 2, 2], BF16, kind="Internal").ap()
    dn_bf = nc.dram_tensor("dn_bf", [12, 2048, 2, 2], BF16, kind="Internal").ap()

    st = contextlib.ExitStack()
    with st:
        kident = st.enter_context(nc.sbuf_tensor("kident", [128, 128], BF16))
        khead = st.enter_context(nc.sbuf_tensor("khead", [128, 128], BF16))
        kmask = st.enter_context(nc.sbuf_tensor("kmask", [128, 1024], BF16))
        kmask01 = st.enter_context(nc.sbuf_tensor("kmask01", [128, 1024], BF16))
        kones = st.enter_context(nc.sbuf_tensor("kones", [128, 512], BF16))
        krsi = st.enter_context(nc.sbuf_tensor("krsi", [1, 64], BF16))
        bias1 = st.enter_context(nc.sbuf_tensor("bias1", [128, 32], F32))
        negc = st.enter_context(nc.sbuf_tensor("negc", [128, 16], F32))
        stats = st.enter_context(nc.sbuf_tensor("stats", [128, 8], F32))
        nsc = [st.enter_context(nc.sbuf_tensor(f"nsc{i}", [128, 12], F32)) for i in range(4)]
        nsci = st.enter_context(nc.sbuf_tensor("nsci", [128, 8], I32))
        keps = st.enter_context(nc.sbuf_tensor("keps", [128, 1], F32))
        qmax = st.enter_context(nc.sbuf_tensor("qmax", [128, 8], F32))
        kqmax = st.enter_context(nc.sbuf_tensor("kqmax", [128, 8], F32))
        ctmp = [st.enter_context(nc.sbuf_tensor(f"ctmp{i}", [128, 8], F32)) for i in range(2)]
        negc8 = st.enter_context(nc.sbuf_tensor("negc8", [128, 8], BF16))

        ABF_COLS = 73792
        AF_COLS = 13568
        abf_t = st.enter_context(nc.sbuf_tensor("abf", [128, ABF_COLS], BF16))
        af_t = st.enter_context(nc.sbuf_tensor("af", [128, AF_COLS], F32))
        A = Arena(abf_t, "bf", ABF_COLS, blk=512)
        Z = Arena(af_t, "f3", AF_COLS, blk=1024)

        ps01 = st.enter_context(nc.psum_tensor("ps01", [128, 1024], F32))
        ps45 = st.enter_context(nc.psum_tensor("ps45", [128, 1024], F32))
        PS = [ps01[:, 0:512], ps01[:, 512:1024]] + [
            st.enter_context(nc.psum_tensor(f"ps{i}", [128, 512], F32)) for i in range(2, 4)] + [
            ps45[:, 0:512], ps45[:, 512:1024]]
        PT = st.enter_context(nc.psum_tensor("pt", [128, 1024], BF16))
        PT2 = st.enter_context(nc.psum_tensor("pt2", [128, 1024], BF16))
        PTS = [(PT, "P:T"), (PT2, "P:T2")]

        p = Prog(nc, st)

        p.op("Y", lambda y: y.dma_start(out=kident[:, :], in_=ident_in), writes=["kident"], dma="dc")
        p.op("Y", lambda y: y.dma_start(out=khead[:, :], in_=headones_in), writes=["khead"], dma="dc")
        p.op("Y", lambda y: y.dma_start(out=kmask[:, :], in_=bigmask_in), writes=["kmask"], dma="dc")
        p.op("Y", lambda y: y.dma_start(out=kmask01[:, :], in_=mask01_in), writes=["kmask01"], dma="dc")
        p.op("V", lambda v: v.memset(kones[:, :], 1.0), writes=["kones"])
        p.op("V", lambda v: v.memset(krsi[:, :], 1.0), writes=["krsi"])
        p.op("V", lambda v: v.memset(keps[:, :], EPS), writes=["keps"])

        A.place("R2", 0, 32768)
        A.place("rowp", 32768, 8192)
        A.place("wtb", 73728, 64)
        A.place("tbuf", 40960, 4096)
        A.place("coefrep", 45056, 4096)
        A.place("coefrep2", 69632, 4096)
        A.place("rawk", 49152, 8192)
        A.place("ttmp0", 57344, 2048)
        A.place("ttmp1", 59392, 2048)
        A.place("stag", 61440, 8192)
        A.place("wes", 40960, 8192)
        A.place("xn", 49152, 4096)
        A.place("xnT", 53248, 4096)
        A.place("xnT2", 36864, 4096)
        A.place("hh", 57344, 16384)
        A.place("wv", 0, 8192)
        A.place("wos", 8192, 8192)
        A.place("wqk", 16384, 4096)
        A.place("axn", 20480, 2048)
        A.place("axnT", 22528, 4096)
        A.place("qs", 26624, 4096)
        A.place("pb2", 30720, 2048)
        A.place("nsq", 30720, 2048)
        A.place("os", 32768, 4096)
        A.place("ks", 36864, 16384)
        A.place("vs", 53248, 16640)
        A.place("axn2", 70144, 2048)
        A.place("pb3", 72192, 1024)

        Z.place("xb", 0, 8192)
        Z.place("yb", 8192, 2048)
        Z.place("fscr", 0, 4096)
        Z.place("fident", 10240, 128)
        Z.place("hcoefU", 10368, 320)
        Z.place("hcoefD", 10688, 320)
        Z.place("xa", 0, 8192)
        Z.place("x2b", 8192, 2048)
        Z.place("srow", 10240, 512)
        Z.place("srep", 10752, 512)
        Z.place("kbrowF", 11264, 2048)
        Z.place("srowRF", 13312, 256)

        rowp = lambda pp, c0, cn: A.view("rowp", pp, 1, c0, cn)
        kbrow = lambda c0, cn: Z.view("kbrowF", 0, 1, 0, 2048).bitcast(BF16)[0:1, c0:c0 + cn]
        kbt = Z.tags("kbrowF")
        PBS = [("pb2", 0), ("pb2", 1024), ("pb3", 0)]
        srow_v = lambda: Z.view("srow", 0, 1, 0, 512)
        srowR_v = lambda: Z.view("srowRF", 0, 1, 0, 256).bitcast(BF16)
        srep_v = lambda: Z.view("srep", 0, 64, 0, 512)

        def load_hcoef(w_dram, name):
            for si, s in enumerate(range(7, 12)):
                p.op("S", lambda y, s=s, si=si, w_dram=w_dram, name=name: y.dma_start(
                    out=Z.view(name, 0, 128, si * 64, 64).rearrange("p (q ab) -> p q ab", ab=4),
                    in_=w_dram[s].rearrange("(q p) a b -> p q (a b)", p=128)),
                    writes=Z.tags(name, si * 64, 64), dma="dhc")

        load_hcoef(up_w_in, "hcoefU")
        load_hcoef(dn_w_in, "hcoefD")

        for src, dst, nm in ((up_w_in, up_bf, "ubf"), (dn_w_in, dn_bf, "dbf")):
            p.op("G", lambda g, src=src, dst=dst: g.dma_start(
                out=dst.rearrange("s k a b -> (s k a b)").unsqueeze(0),
                in_=src.rearrange("s k a b -> (s k a b)").unsqueeze(0)),
                writes=[f"dram_{nm}"], dma="dpc")

        def pack_qk():
            for pr in range(8):
                for hh in range(2):
                    for src, dst, nm in ((wq_in, wqd, "wq"), (wk_in, wkd, "wk")):
                        p.op("G", lambda g, src=src, dst=dst, pr=pr, hh=hh: g.dma_start(
                            out=dst[pr][:, :, 64 * hh:64 * hh + 64],
                            in_=src[2 * pr + hh].rearrange("(c p) d -> p c d", p=128)),
                            writes=[f"dram_{nm}:{pr}:{hh}"], dma=f"dpk{pr % 2}")

        def build_T(w_dram, wtag, transposed, reverse):
            p.op("Y", lambda y: y.dma_start(out=A.view("tbuf"), in_=tident_in),
                 writes=A.tags("tbuf"), dma="dc")
            order = list(range(7))
            if reverse:
                order = order[::-1]
            i01, i10 = (2, 1) if transposed else (1, 2)
            for st_i in order:
                s = 1 << st_i
                snm = "rowp" if st_i % 2 == 0 else "stag"
                for lh in range(2):
                    p.op("Y", lambda y, st_i=st_i, lh=lh, snm=snm, w_dram=w_dram: y.dma_start(
                        out=A.view(snm, 0, 1, lh * 4096, 4096),
                        in_=w_dram[st_i].rearrange("k a b -> (k a b)").unsqueeze(0)[0:1, lh * 4096:(lh + 1) * 4096]),
                        reads=[wtag],
                        writes=A.tags(snm, lh * 4096, 4096), dma="dcf")
                for half in range(2):
                    crnm = "coefrep" if half == 0 else "coefrep2"
                    for b in range(8):
                        pv, ptag = PS[b % 6], f"P:{b % 6}"
                        p.op("T", lambda t, half=half, b=b, snm=snm, pv=pv: t.matmul(
                            pv[:, :], kones[0:1, 0:128],
                            A.view(snm, 0, 1, half * 4096 + b * 512, 512), start=True, stop=True),
                            reads=A.tags(snm, half * 4096 + b * 512, 512) + ["kones"], writes=[ptag])
                        p.op("S", lambda s_, b=b, crnm=crnm, pv=pv: s_.copy(
                            A.view(crnm, 0, 128, b * 512, 512), pv[:, :]),
                            reads=[ptag], writes=A.tags(crnm, b * 512, 512))
                    f0 = half * 2048
                    tags_t = A.tags("tbuf", f0, 2048)
                    tags_c = A.tags(crnm)

                    def A_v(st_i=st_i, s=s, f0=f0):
                        tv = A.view("tbuf", 0, 128, f0, 2048)
                        tv4 = tv.rearrange("p (nb two s) -> p nb two s", two=2, s=s)
                        return tv4[:, :, 0, :]

                    def B_v(st_i=st_i, s=s, f0=f0):
                        tv = A.view("tbuf", 0, 128, f0, 2048)
                        tv4 = tv.rearrange("p (nb two s) -> p nb two s", two=2, s=s)
                        return tv4[:, :, 1, :]

                    def W_v(j, s=s, crnm=crnm):
                        cr = A.view(crnm)
                        cr4 = cr.rearrange("p (nb s four) -> p nb s four", s=s, four=4)
                        return cr4[:, :, :, j]

                    def T_v(name, s=s):
                        return A.view(name, 0, 128, 0, 1024).rearrange("p (nb s) -> p nb s", s=s)

                    p.op("V", lambda v, B_v=B_v, W_v=W_v, T_v=T_v, i01=i01: v.tensor_tensor(
                        T_v("ttmp0"), B_v(), W_v(i01), ALU.mult),
                        reads=tags_t + tags_c, writes=A.tags("ttmp0", 0, 1024))
                    p.op("V", lambda v, B_v=B_v, W_v=W_v, T_v=T_v: v.tensor_tensor(
                        T_v("ttmp1"), B_v(), W_v(3), ALU.mult),
                        reads=tags_t + tags_c, writes=A.tags("ttmp1", 0, 1024))
                    p.op("V", lambda v, A_v=A_v, B_v=B_v, W_v=W_v, i10=i10: v.tensor_tensor(
                        B_v(), A_v(), W_v(i10), ALU.mult),
                        reads=tags_t + tags_c, writes=tags_t)
                    p.op("V", lambda v, B_v=B_v, T_v=T_v: v.tensor_tensor(
                        B_v(), B_v(), T_v("ttmp1"), ALU.add),
                        reads=tags_t + A.tags("ttmp1", 0, 1024), writes=tags_t)
                    p.op("V", lambda v, A_v=A_v, W_v=W_v: v.tensor_tensor(
                        A_v(), A_v(), W_v(0), ALU.mult),
                        reads=tags_t + tags_c, writes=tags_t)
                    p.op("V", lambda v, A_v=A_v, T_v=T_v: v.tensor_tensor(
                        A_v(), A_v(), T_v("ttmp0"), ALU.add),
                        reads=tags_t + A.tags("ttmp0", 0, 1024), writes=tags_t)

        def high_stages(tags_fn, view_fn, coef, transposed, reverse, width):
            order = list(range(5))
            if reverse:
                order = order[::-1]
            i01, i10 = (2, 1) if transposed else (1, 2)
            for si in order:
                stride = 1 << (7 + si)
                cstep = stride // 128
                pi = 0
                for c in range(32):
                    if c & cstep:
                        continue
                    c2 = c + cstep
                    col = (si * 16 + pi) * 4
                    pi += 1
                    w = lambda j, col=col, coef=coef: Z.view(coef, 0, 128, col + j, 1)
                    zt = Z.tags(coef, col, 4)
                    ta, tb_ = tags_fn(c), tags_fn(c2)
                    eng = "V"
                    tt0 = A.tags("ttmp0", 0, width) if width > 1 else A.tags("ttmp0", 0, 1)
                    tt1 = A.tags("ttmp1", 0, width) if width > 1 else A.tags("ttmp1", 0, 1)
                    p.op(eng, lambda v, c2=c2, w=w, i01=i01, width=width: v.tensor_scalar(
                        A.view("ttmp0", 0, 128, 0, width), view_fn(c2), w(i01), None, ALU.mult),
                        reads=tb_ + zt, writes=tt0)
                    p.op(eng, lambda v, c2=c2, w=w, width=width: v.tensor_scalar(
                        A.view("ttmp1", 0, 128, 0, width), view_fn(c2), w(3), None, ALU.mult),
                        reads=tb_ + zt, writes=tt1)
                    p.op(eng, lambda v, c=c, c2=c2, w=w, i10=i10, width=width: v.scalar_tensor_tensor(
                        view_fn(c2), view_fn(c), w(i10),
                        A.view("ttmp1", 0, 128, 0, width), ALU.mult, ALU.add),
                        reads=ta + tt1 + zt, writes=tb_)
                    p.op(eng, lambda v, c=c, w=w, width=width: v.scalar_tensor_tensor(
                        view_fn(c), view_fn(c), w(0),
                        A.view("ttmp0", 0, 128, 0, width), ALU.mult, ALU.add),
                        reads=ta + tt0 + zt, writes=ta)

        build_T(up_bf, "dram_ubf", transposed=False, reverse=False)

        p.op("G", lambda g: g.dma_start(out=rowp(32, 0, 4096), in_=b_exp_in.unsqueeze(0)),
             writes=A.tags("rowp"), dma="dcf")
        for cg in range(4):
            p.op("T", [
                (lambda t, c=8 * cg + j: t.transpose(
                    PT[:, 2 * c:2 * c + 1], rowp(32, 128 * c, 128), kident[32:33, 32:33]))
                for j in range(8)],
                reads=A.tags("rowp") + ["kident"], writes=["P:T"])
        p.op("V", lambda v: v.tensor_copy(
            A.view("wtb", 0, 128, 0, 32),
            PT[:, 0:64].rearrange("p (a b) -> p a b", b=2)[:, :, 0]),
            reads=["P:T"], writes=A.tags("wtb"))

        for k in range(8):
            kb = k % 2
            p.op("G", lambda g, k=k, kb=kb: g.dma_start(
                out=A.view("rawk", 0, 128, kb * 4096, 4096),
                in_=w_exp_in.rearrange("(c p) f -> c p f", p=128)[k]),
                writes=A.tags("rawk", kb * 4096, 4096), dma=f"dre{kb}")
            for cg in range(4):
                ptile, pttag = PTS[cg % 2]
                p.op("T", [
                    (lambda t, kb=kb, c=8 * cg + j, j=j, ptile=ptile: t.transpose(
                        ptile[:, 128 * j:128 * (j + 1)],
                        A.view("rawk", 0, 128, kb * 4096 + 128 * c, 128),
                        kident[:, :]))
                    for j in range(8)],
                    reads=A.tags("rawk", kb * 4096, 4096) + ["kident"], writes=[pttag])
                p.op("V", lambda v, k=k, cg=cg, ptile=ptile: v.tensor_copy(
                    A.view("R2").rearrange("p (c q) -> p c q", q=1024)[:, 8 * cg:8 * cg + 8, 128 * k:128 * (k + 1)],
                    ptile[:, :].rearrange("p (j q) -> p j q", q=128)),
                    reads=[pttag],
                    writes=[t for j in range(8) for t in A.tags("R2", (8 * cg + j) * 1024 + 128 * k, 128)])

        for c in range(32):
            for half in range(2):
                pv = PS[2 + (c * 2 + half) % 4]
                ptag = f"P:{2 + (c * 2 + half) % 4}"
                p.op("T", lambda t, c=c, half=half, pv=pv: t.matmul(
                    pv[:, :], A.view("tbuf", 0, 128, 128 * c, 128),
                    A.view("R2", 0, 128, c * 1024 + half * 512, 512),
                    start=True, stop=True),
                    reads=A.tags("tbuf", 128 * c, 128) + A.tags("R2", c * 1024 + half * 512, 512),
                    writes=[ptag])
                p.op("S", lambda s_, c=c, half=half, pv=pv: s_.copy(
                    A.view("R2", 0, 128, c * 1024 + half * 512, 512), pv[:, :]),
                    reads=[ptag], writes=A.tags("R2", c * 1024 + half * 512, 512))
        p.op("T", [
            (lambda t, c=c: t.matmul(PS[4][:, c:c + 1],
                                     A.view("tbuf", 0, 128, 128 * c, 128),
                                     A.view("wtb", 0, 128, c, 1), start=True, stop=True))
            for c in range(32)],
            reads=A.tags("tbuf") + A.tags("wtb"), writes=["P:4"])
        p.op("V", lambda v: v.tensor_copy(A.view("wtb", 0, 128, 0, 32), PS[4][:, 0:32]),
             reads=["P:4"], writes=A.tags("wtb"))

        high_stages(lambda c: A.tags("R2", c * 1024, 1024),
                    lambda c: A.view("R2", 0, 128, c * 1024, 1024),
                    "hcoefU", transposed=False, reverse=False, width=1024)
        high_stages(lambda c: A.tags("wtb"),
                    lambda c: A.view("wtb", 0, 128, c, 1),
                    "hcoefU", transposed=False, reverse=False, width=1)

        for k in range(8):
            kb = k % 2
            for cg in range(4):
                ptile, pttag = PTS[cg % 2]
                p.op("T", [
                    (lambda t, k=k, c=8 * cg + j, j=j, ptile=ptile: t.transpose(
                        ptile[:, 128 * j:128 * (j + 1)],
                        A.view("R2", 0, 128, c * 1024 + 128 * k, 128),
                        kident[:, :]))
                    for j in range(8)],
                    reads=[tg for j in range(8) for tg in A.tags("R2", (8 * cg + j) * 1024 + 128 * k, 128)] + ["kident"],
                    writes=[pttag])
                p.op("V", lambda v, kb=kb, cg=cg, ptile=ptile: v.tensor_copy(
                    A.view("stag", 0, 128, kb * 4096 + cg * 1024, 1024), ptile[:, :]),
                    reads=[pttag], writes=A.tags("stag", kb * 4096 + cg * 1024, 1024))
            p.op("G", lambda g, k=k, kb=kb: g.dma_start(
                out=wexp_s[:, :, k, :].rearrange("m r f -> r m f"),
                in_=A.view("stag", 0, 128, kb * 4096, 4096).rearrange("r (m f) -> r m f", f=128)),
                reads=A.tags("stag", kb * 4096, 4096), writes=[f"dram_we:{k}"], dma=f"dse{kb}")

        p.op("G", lambda g: g.dma_start(out=Z.view("fscr", 0, 1, 0, 4096), in_=up_b_in.unsqueeze(0)),
             writes=Z.tags("fscr"), dma="dcf")
        p.op("V", lambda v: v.tensor_copy(Z.view("fident"), kident[:, :]),
             reads=["kident"], writes=Z.tags("fident"))
        for cg in range(4):
            p.op("T", [
                (lambda t, c=8 * cg + j, j=j: t.transpose(
                    PS[5][:, j:j + 1], Z.view("fscr", 0, 1, 128 * c, 128), Z.view("fident", 0, 1, 0, 1)))
                for j in range(8)],
                reads=Z.tags("fscr") + Z.tags("fident"), writes=["P:5"])
            p.op("V", lambda v, cg=cg: v.tensor_tensor(
                bias1[:, 8 * cg:8 * cg + 8], PS[5][:, 0:8],
                A.view("wtb", 0, 128, 8 * cg, 8), ALU.add),
                reads=["P:5"] + A.tags("wtb"), writes=["bias1"])

        for c in range(32):
            p.op("G", lambda g, c=c: g.dma_start(
                out=A.view("R2", 0, 128, c * 1024, 1024),
                in_=w_con_in.rearrange("(c p) d -> c p d", p=128)[c]),
                writes=A.tags("R2", c * 1024, 1024), dma=f"drc{c % 2}")
        high_stages(lambda c: A.tags("R2", c * 1024, 1024),
                    lambda c: A.view("R2", 0, 128, c * 1024, 1024),
                    "hcoefD", transposed=True, reverse=True, width=1024)
        build_T(dn_bf, "dram_dbf", transposed=True, reverse=True)
        for c in range(32):
            for half in range(2):
                pv = PS[2 + (c * 2 + half) % 4]
                ptag = f"P:{2 + (c * 2 + half) % 4}"
                p.op("T", lambda t, c=c, half=half, pv=pv: t.matmul(
                    pv[:, :], A.view("tbuf", 0, 128, 128 * c, 128),
                    A.view("R2", 0, 128, c * 1024 + half * 512, 512),
                    start=True, stop=True),
                    reads=A.tags("tbuf", 128 * c, 128) + A.tags("R2", c * 1024 + half * 512, 512),
                    writes=[ptag])
                p.op("S", lambda s_, c=c, half=half, pv=pv: s_.copy(
                    A.view("R2", 0, 128, c * 1024 + half * 512, 512), pv[:, :]),
                    reads=[ptag], writes=A.tags("R2", c * 1024 + half * 512, 512))
        for c in range(32):
            p.op("G", lambda g, c=c: g.dma_start(
                out=wcon_s[c], in_=A.view("R2", 0, 128, c * 1024, 1024)),
                reads=A.tags("R2", c * 1024, 1024), writes=[f"dram_wc:{c}"], dma=f"dsc{c % 2}")

        def bias2_setup():
            p.op("G", lambda g: g.dma_start(out=Z.view("fscr", 0, 1, 0, 1024), in_=b_con_in.unsqueeze(0)),
                 writes=Z.tags("fscr"), dma="dcf")
            p.op("G", lambda g: g.dma_start(out=Z.view("fscr", 0, 1, 1024, 1024), in_=dn_b_in.unsqueeze(0)),
                 writes=Z.tags("fscr"), dma="dcf")
            p.op("V", lambda v: v.tensor_tensor(
                rowp(0, 0, 1024), Z.view("fscr", 0, 1, 0, 1024),
                Z.view("fscr", 0, 1, 1024, 1024), ALU.add),
                reads=Z.tags("fscr"), writes=A.tags("rowp"))

        bias2_setup()

        def newton_rsqrt(scalar_ok=False):
            if scalar_ok:
                p.op("S", lambda s_: s_.activation(nsc[0][:, 0:4], stats[:, 4:8], AF.Ln,
                                                   bias=keps[:, 0:1], scale=1.0),
                     reads=["stats", "keps"], writes=["nsc0"])
                p.op("S", lambda s_: s_.activation(stats[:, 4:8], nsc[0][:, 0:4], AF.Exp, scale=-0.5),
                     reads=["nsc0"], writes=["stats"])
                return
            v4 = lambda t: t[:, 0:4]
            p.op("V", lambda v: v.tensor_scalar(v4(nsc[0]), stats[:, 4:8], EPS, None, ALU.add),
                 reads=["stats"], writes=["nsc0"])
            p.op("V", lambda v: v.tensor_single_scalar(
                v4(nsci), v4(nsc[0]).bitcast(I32), 1, ALU.logical_shift_right),
                reads=["nsc0"], writes=["nsci"])
            p.op("V", lambda v: v.tensor_scalar(
                v4(nsci), v4(nsci), -1, 0x5f3759df, ALU.mult, ALU.add),
                reads=["nsci"], writes=["nsci"])
            p.op("V", lambda v: v.tensor_copy(v4(nsc[1]), v4(nsci).bitcast(F32)),
                 reads=["nsci"], writes=["nsc1"])
            for _ in range(3):
                p.op("V", lambda v: v.tensor_tensor(v4(nsc[2]), v4(nsc[1]), v4(nsc[1]), ALU.mult),
                     reads=["nsc1"], writes=["nsc2"])
                p.op("V", lambda v: v.tensor_tensor(v4(nsc[2]), v4(nsc[2]), v4(nsc[0]), ALU.mult),
                     reads=["nsc2", "nsc0"], writes=["nsc2"])
                p.op("V", lambda v: v.tensor_scalar(v4(nsc[2]), v4(nsc[2]), -0.5, 1.5, ALU.mult, ALU.add),
                     reads=["nsc2"], writes=["nsc2"])
                p.op("V", lambda v: v.tensor_tensor(v4(nsc[1]), v4(nsc[1]), v4(nsc[2]), ALU.mult),
                     reads=["nsc1", "nsc2"], writes=["nsc1"])
            p.op("V", lambda v: v.tensor_copy(stats[:, 4:8], v4(nsc[1])),
                 reads=["nsc1"], writes=["stats"])

        def emit_ln_tr(xsrc_view, xsrc_tags, xslots, xnT_name, rsqrt_scalar=False):
            nslots = len(xslots)
            for t in range(4):
                p.op("V", lambda v, t=t: v.bn_stats(nsc[3][:, 0:6], xsrc_view(1024 * t, 512)),
                     reads=xsrc_tags(1024 * t, 512), writes=["nsc3"])
                p.op("V", lambda v, t=t: v.bn_stats(nsc[3][:, 6:12], xsrc_view(1024 * t + 512, 512)),
                     reads=xsrc_tags(1024 * t + 512, 512), writes=["nsc3"])
                p.op("V", lambda v, t=t: v.bn_aggr(
                    stats[:, :].rearrange("p (a b) -> p a b", b=4)[:, :, t], nsc[3][:, 0:12]),
                    reads=["nsc3"], writes=["stats"])
            newton_rsqrt(scalar_ok=rsqrt_scalar)

            def emit_norm(t):
                nm, sl = xslots[t % nslots]
                p.op("V", lambda v, t=t, nm=nm, sl=sl: v.tensor_scalar(
                    A.view(nm, 0, 128, sl, 1024), xsrc_view(1024 * t, 1024),
                    stats[:, t:t + 1], stats[:, 4 + t:5 + t], ALU.subtract, ALU.mult),
                    reads=xsrc_tags(1024 * t, 1024) + ["stats"],
                    writes=A.tags(nm, sl, 1024))

            def emit_tr(t):
                nm, sl = xslots[t % nslots]
                ptile, pttag = PTS[t % 2]
                p.op("T", [
                    (lambda tt, t=t, nm=nm, sl=sl, c=c, ptile=ptile: tt.transpose(
                        ptile[:, 128 * c:128 * (c + 1)],
                        A.view(nm, 0, 128, sl + 128 * c, 128), kident[:, :]))
                    for c in range(8)],
                    reads=A.tags(nm, sl, 1024) + ["kident"], writes=[pttag])
                p.op("V", lambda e, t=t, xnT_name=xnT_name, ptile=ptile: e.tensor_copy(
                    A.view(xnT_name).rearrange("p (c q) -> p c q", q=512)[:, :, 128 * t:128 * (t + 1)],
                    ptile[:, :].rearrange("p (c q) -> p c q", q=128)),
                    reads=[pttag],
                    writes=[tg for c in range(8) for tg in A.tags(xnT_name, 512 * c + 128 * t, 128)])

            if nslots >= 4:
                for t in range(4):
                    emit_norm(t)
                for t in range(4):
                    emit_tr(t)
            else:
                for t in range(4):
                    emit_norm(t)
                    emit_tr(t)

        def emit_ffn(src_d, dst_d, tag_pfx):
            def load_x(s):
                sb = s % 2
                p.op("Y", lambda y, s=s, sb=sb: y.dma_start(
                    out=Z.view("xb", 0, 128, sb * 4096, 4096).rearrange("p (t f) -> p t f", f=1024),
                    in_=src_d[SUP * s:SUP * (s + 1)].rearrange("(t p) f -> p t f", p=128)),
                    writes=Z.tags("xb", sb * 4096, 4096), dma=f"dx{sb}")

            def ln_super(s):
                sb = s % 2
                xv_ = lambda c0, cn, sb=sb: Z.view("xb", 0, 128, sb * 4096 + c0, cn)
                xt_ = lambda c0, cn, sb=sb: Z.tags("xb", sb * 4096 + c0, cn)
                emit_ln_tr(xv_, xt_, [("xn", i * 1024) for i in range(4)],
                           "xnT" if s % 2 == 0 else "xnT2")
                return xv_, xt_

            load_x(0)
            xv, xt = ln_super(0)
            for s in range(NSUP):
                sb = s % 2
                xnT_nm = "xnT" if s % 2 == 0 else "xnT2"
                if s + 1 < NSUP:
                    load_x(s + 1)

                def wes_load(qd):
                    sl = (4 * qd) % 8
                    p.op("G", lambda g, qd=qd, sl=sl: g.dma_start(
                        out=A.view("wes", 0, 128, sl * 1024, 4096).rearrange("r (m kf) -> r m kf", kf=1024),
                        in_=wexp_s[4 * qd:4 * qd + 4].rearrange("m r k f -> r m (k f)")),
                        reads=[f"dram_we:{k}" for k in range(8)],
                        writes=A.tags("wes", sl * 1024, 4096), dma=f"dwe{qd % 2}")

                wes_load(0)
                wes_load(1)

                for m in range(32):
                    pv, ptag = PS[m % 2], f"P:{m % 2}"
                    p.op("T", [
                        (lambda t, m=m, k=k, pv=pv, xnT_nm=xnT_nm: t.matmul(
                            pv[:, :],
                            A.view("wes", 0, 128, (m % 8) * 1024 + 128 * k, 128),
                            A.view(xnT_nm, 0, 128, 512 * k, 512),
                            start=(k == 0), stop=(k == 7)))
                        for k in range(8)],
                        reads=A.tags("wes", (m % 8) * 1024, 1024) + A.tags(xnT_nm),
                        writes=[ptag])
                    if m % 4 == 3 and (m + 5) // 4 < 8:
                        wes_load((m + 5) // 4)
                    p.op("S", lambda s_, m=m, pv=pv: s_.activation(
                        A.view("hh", 0, 128, 512 * m, 512), pv[:, :], AF.Gelu,
                        bias=bias1[:, m:m + 1], scale=1.0),
                        reads=[ptag, "bias1"], writes=A.tags("hh", 512 * m, 512))

                xv_cur, xt_cur = xv, xt
                if s + 1 < NSUP:
                    xv, xt = ln_super(s + 1)

                for t in range(4):
                    for half in range(2):
                        pv = PS[2 + (t * 2 + half) % 2]
                        ptag = f"P:{2 + (t * 2 + half) % 2}"
                        blds = [
                            (lambda tt, k=k, t=t, half=half, pv=pv: tt.matmul(
                                pv[:, :],
                                A.view("hh", 0, 128, 512 * k + 128 * t, 128),
                                A.view("R2", 0, 128, 1024 * k + 512 * half, 512),
                                start=(k == 0), stop=False))
                            for k in range(32)]
                        blds.append(lambda tt, half=half, pv=pv: tt.matmul(
                            pv[:, :], kones[0:1, 0:128], rowp(0, half * 512, 512),
                            start=False, stop=True))
                        p.op("T", blds,
                             reads=A.tags("hh") + A.tags("R2") + ["kones"] + A.tags("rowp"),
                             writes=[ptag])
                        yb = t % 2
                        p.op("V", lambda v, t=t, half=half, pv=pv, yb=yb, xv=xv_cur: v.scalar_tensor_tensor(
                            Z.view("yb", 0, 128, yb * 1024 + half * 512, 512),
                            pv[:, :], 0.5, xv(1024 * t + 512 * half, 512),
                            ALU.mult, ALU.add),
                            reads=[ptag] + xt_cur(1024 * t + 512 * half, 512),
                            writes=Z.tags("yb", yb * 1024 + half * 512, 512))
                    yb = t % 2
                    eng = "G" if t % 2 == 0 else "Y"
                    p.op(eng, lambda e, s=s, t=t, yb=yb: e.dma_start(
                        out=dst_d[SUP * s + 128 * t:SUP * s + 128 * (t + 1)],
                        in_=Z.view("yb", 0, 128, yb * 1024, 1024)),
                        reads=Z.tags("yb", yb * 1024, 1024), dma=f"dy{yb}")

        emit_ffn(x_in, x1_d, "f1")
        pack_qk()
        p.fence()

        for c in range(8):
            p.op("G", lambda g, c=c: g.dma_start(
                out=A.view("wv", 0, 128, 1024 * c, 1024).rearrange("p (h d) -> p h d", d=64),
                in_=wv_in.rearrange("h (c p) d -> c p h d", p=128)[c]),
                writes=A.tags("wv", 1024 * c, 1024), dma="dwv")
        for c in range(8):
            p.op("G", lambda g, c=c: g.dma_start(
                out=A.view("wos", 0, 128, 1024 * c, 1024),
                in_=wo_in.rearrange("(c p) d -> c p d", p=128)[c]),
                writes=A.tags("wos", 1024 * c, 1024), dma="dwo")
        p.op("G", lambda g: g.dma_start(out=kbrow(0, 1024),
                                        in_=bq_in.rearrange("h d -> (h d)").unsqueeze(0)),
             writes=kbt, dma="dcf")
        p.op("G", lambda g: g.dma_start(out=kbrow(1024, 1024),
                                        in_=bk_in.rearrange("h d -> (h d)").unsqueeze(0)),
             writes=kbt, dma="dcf")
        p.op("G", lambda g: g.dma_start(out=kbrow(2048, 1024),
                                        in_=bv_in.rearrange("h d -> (h d)").unsqueeze(0)),
             writes=kbt, dma="dcf")
        p.op("G", lambda g: g.dma_start(out=kbrow(3072, 1024), in_=bo_in.unsqueeze(0)),
             writes=kbt, dma="dcf")
        p.op("V", lambda v: v.memset(
            A.view("vs").rearrange("p (t h d) -> p t h d", t=16, d=65)[:, :, :, 64], 1.0),
            writes=A.tags("vs"))

        for q in range(NSEQ):
            base = q * SEQ
            xnT_v = lambda c0, cn: A.view("axnT", 0, 128, c0, cn)
            xnT_t = A.tags("axnT")
            PTf, PT2f = PT.bitcast(F32), PT2.bitcast(F32)

            def dxa_issue(u, base=base):
                xo = (u % 2) * 4096
                p.op("Y", lambda y, base=base, u=u, xo=xo: y.dma_start(
                    out=Z.view("xa", 0, 128, xo, 4096).rearrange("p (t f) -> p t f", f=1024),
                    in_=x1_d[base + SUP * u: base + SUP * (u + 1)].rearrange("(t p) f -> p t f", p=128)),
                    writes=Z.tags("xa", xo, 4096), dma=f"dxa{u % 2}")

            def ln_u(u):
                xo = (u % 2) * 4096
                xav = lambda c0, cn, xo=xo: Z.view("xa", 0, 128, xo + c0, cn)
                xat = lambda c0, cn, xo=xo: Z.tags("xa", xo + c0, cn)
                emit_ln_tr(xav, xat, [("axn", 0), ("axn", 1024), ("axn2", 0), ("axn2", 1024)],
                           "axnT", rsqrt_scalar=True)

            def wqk_load(pi):
                sl = (pi % 2) * 2048
                dram, nm = (wqd, "wq") if pi < 4 else (wkd, "wk")
                pr0 = (pi % 4) * 2
                p.op("G", lambda g, dram=dram, pr0=pr0, sl=sl: g.dma_start(
                    out=A.view("wqk", 0, 128, sl, 2048).rearrange("p (w cd) -> p w cd", cd=1024),
                    in_=dram[pr0:pr0 + 2].rearrange("w p c d -> p w (c d)")),
                    reads=[f"dram_{nm}:{pr}:{hh}" for pr in (pr0, pr0 + 1) for hh in (0, 1)],
                    writes=A.tags("wqk", sl, 2048), dma=f"dwqk{pi % 2}")

            def qk_group(pi, pp, uu):
                isk = pi >= 4
                bcol = 1024 if isk else 0
                pr = (pi % 4) * 2 + pp
                sl = (pi % 2) * 2048 + pp * 1024
                if isk:
                    pv, ptag = (PTf, "P:T") if pr % 2 == 0 else (PT2f, "P:T2")
                else:
                    pv, ptag = PS[pr % 2], f"P:{pr % 2}"
                blds = [
                    (lambda t, sl=sl, k=k, pv=pv: t.matmul(
                        pv[:, :],
                        A.view("wqk", 0, 128, sl + 128 * k, 128),
                        xnT_v(512 * k, 512), start=(k == 0), stop=False))
                    for k in range(8)]
                blds.append(lambda t, bcol=bcol, pr=pr, pv=pv: t.matmul(
                    pv[:, :], kbrow(bcol + 128 * pr, 128),
                    kones[0:1, 0:512], start=False, stop=True))
                p.op("T", blds,
                     reads=A.tags("wqk", sl, 1024) + xnT_t + kbt + ["kones"], writes=[ptag])
                if isk:
                    p.op("V", lambda v, pr=pr, uu=uu, pv=pv: v.tensor_copy(
                        A.view("ks", 0, 128, 2048 * pr + 512 * uu, 512), pv[:, :]),
                        reads=[ptag], writes=A.tags("ks", 2048 * pr + 512 * uu, 512))
                else:
                    p.op("S", lambda s_, pr=pr, pv=pv: s_.copy(
                        A.view("qs", 0, 128, 512 * pr, 512), pv[:, :]),
                        reads=[ptag], writes=A.tags("qs", 512 * pr, 512))

            def v_group(t, half, uu):
                pv, ptag = ((PTf, "P:T") if half == 0 else (PT2f, "P:T2"))
                blds = [
                    (lambda tt, t=t, half=half, k=k, pv=pv: tt.matmul(
                        pv[:, :], xnT_v(512 * k + 128 * t, 128),
                        A.view("wv", 0, 128, 1024 * k + 512 * half, 512),
                        start=(k == 0), stop=False))
                    for k in range(8)]
                blds.append(lambda tt, half=half, pv=pv: tt.matmul(
                    pv[:, :], kones[0:1, 0:128],
                    kbrow(2048 + half * 512, 512),
                    start=False, stop=True))
                p.op("T", blds,
                     reads=A.tags("wv") + xnT_t + kbt + ["kones"], writes=[ptag])
                p.op("V", lambda v, uu=uu, t=t, half=half, pv=pv: v.tensor_copy(
                    A.view("vs", 0, 128, 1040 * (4 * uu + t) + 520 * half, 520
                           ).rearrange("p (h d) -> p h d", d=65)[:, :, 0:64],
                    pv[:, :].rearrange("p (h d) -> p h d", d=64)),
                    reads=[ptag],
                    writes=A.tags("vs", 1040 * (4 * uu + t) + 520 * half, 520))

            def kv_items(uu):
                items = [lambda: wqk_load(4), lambda: wqk_load(5)]
                for pi in (4, 5, 6, 7):
                    for pp in (0, 1):
                        items.append(lambda pi=pi, pp=pp: qk_group(pi, pp, uu))
                    if pi + 2 < 8:
                        items.append(lambda pi=pi: wqk_load(pi + 2))
                for t in range(4):
                    for half in range(2):
                        items.append(lambda t=t, half=half: v_group(t, half, uu))
                return items

            dxa_issue(0)
            dxa_issue(1)
            ln_u(0)
            for u in range(4):
                xo = (u % 2) * 4096
                wqk_load(0)
                wqk_load(1)
                for pi in range(4):
                    for pp in range(2):
                        qk_group(pi, pp, u)
                    if pi + 2 < 4:
                        wqk_load(pi + 2)
                if u == 0:
                    for it in kv_items(0):
                        it()
                for mat, dstt, dsttag, fresh, off in (
                        ("qs", qmax, "qmax", True, lambda pr: 512 * pr),
                        ("ks", kqmax, "kqmax", u == 0, lambda pr, u=u: 2048 * pr + 512 * u)):
                    def nsq_sq(pr, mat=mat, off=off):
                        so = (pr % 4) * 512
                        p.op("V", lambda v, mat=mat, pr=pr, so=so, off=off: v.tensor_tensor(
                            A.view("nsq", 0, 128, so, 512), A.view(mat, 0, 128, off(pr), 512),
                            A.view(mat, 0, 128, off(pr), 512), ALU.mult),
                            reads=A.tags(mat, off(pr), 512), writes=A.tags("nsq", so, 512))

                    for pr in range(4):
                        nsq_sq(pr)
                    for pr in range(8):
                        so = (pr % 4) * 512
                        pv, ptag = PS[4 + pr % 2], f"P:{4 + pr % 2}"
                        p.op("T", lambda t, so=so, pv=pv: t.matmul(
                            pv[:, :], khead[:, :], A.view("nsq", 0, 128, so, 512),
                            start=True, stop=True),
                            reads=A.tags("nsq", so, 512) + ["khead"], writes=[ptag])
                        if pr + 4 < 8:
                            nsq_sq(pr + 4)
                        if fresh:
                            p.op("V", lambda v, dstt=dstt, pr=pr, pv=pv: v.tensor_reduce(
                                dstt[:, pr:pr + 1], pv[:, :], AX.X, ALU.max),
                                reads=[ptag], writes=[dsttag])
                        else:
                            p.op("V", lambda v, pv=pv: v.tensor_reduce(
                                ctmp[0][:, 0:1], pv[:, :], AX.X, ALU.max),
                                reads=[ptag], writes=["ctmp0"])
                            p.op("V", lambda v, dstt=dstt, pr=pr: v.tensor_tensor(
                                dstt[:, pr:pr + 1], dstt[:, pr:pr + 1], ctmp[0][:, 0:1], ALU.max),
                                reads=[dsttag, "ctmp0"], writes=[dsttag])
                p.op("S", lambda s_: s_.activation(ctmp[0][:, 0:8], qmax[:, 0:8], AF.Ln),
                     reads=["qmax"], writes=["ctmp0"])
                p.op("S", lambda s_: s_.activation(ctmp[1][:, 0:8], kqmax[:, 0:8], AF.Ln),
                     reads=["kqmax"], writes=["ctmp1"])
                p.op("V", lambda v: v.tensor_tensor(ctmp[0][:, 0:8], ctmp[0][:, 0:8], ctmp[1][:, 0:8], ALU.add),
                     reads=["ctmp0", "ctmp1"], writes=["ctmp0"])
                p.op("S", lambda s_: s_.activation(ctmp[1][:, 0:8], ctmp[0][:, 0:8], AF.Exp, scale=0.5),
                     reads=["ctmp0"], writes=["ctmp1"])
                p.op("V", lambda v: v.tensor_scalar(negc8[:, 0:8], ctmp[1][:, 0:8], -0.125, None, ALU.mult),
                     reads=["ctmp1"], writes=["negc8"])
                for hf in range(2):
                    pv, ptag = PS[4 + hf], f"P:{4 + hf}"
                    p.op("T", lambda t, hf=hf, pv=pv: t.matmul(
                        pv[:, 0:8], kones[64 * hf:64 * hf + 1, 0:128],
                        negc8[64 * hf:64 * hf + 1, 0:8], start=True, stop=True),
                        reads=["negc8", "kones"], writes=[ptag])
                    p.op("V", lambda v, hf=hf, pv=pv: v.tensor_copy(
                        negc.rearrange("p (a b) -> p a b", b=2)[:, :, hf], pv[:, 0:8]),
                        reads=[ptag], writes=["negc"])

                if u < 3:
                    ln_u(u + 1)
                    fill = kv_items(u + 1)
                else:
                    fill = []

                srow64_v = lambda: Z.view("srow", 64, 1, 0, 512)
                srowR64_v = lambda: Z.view("srowRF", 64, 1, 0, 256).bitcast(BF16)

                def tail_a(po, otag):
                    p.op("S", lambda s_, po=po: s_.activation(
                        srow64_v(), po[64:65, :], AF.Ln),
                        reads=[otag], writes=Z.tags("srow"))

                    def _rexp(s_):
                        with nc.allow_low_precision(reason="bf16 softmax recip (exp(-ln(sums))) validated in golden model"):
                            return s_.activation(srowR64_v(), srow64_v(), AF.Exp, scale=-1.0)
                    p.op("S", _rexp,
                         reads=Z.tags("srow"), writes=Z.tags("srowRF"))

                def tail_b(pr, hf, po, otag):
                    p.op("T", lambda t: t.matmul(
                        PS[4][0:64, :], kones[64:65, 0:64], srowR64_v(), start=True, stop=True),
                        reads=Z.tags("srowRF") + ["kones"], writes=["P:4"])
                    p.op("V", lambda v: v.tensor_copy(srep_v(), PS[4][0:64, :]),
                         reads=["P:4"], writes=Z.tags("srep"))
                    p.op("V", lambda v, pr=pr, hf=hf, po=po, otag=otag: v.tensor_tensor(
                        A.view("os", 64 * hf, 64, 512 * pr, 512), po[0:64, :],
                        srep_v(), ALU.mult),
                        reads=[otag] + Z.tags("srep"), writes=A.tags("os", 512 * pr, 512))

                pend_tail = None
                for h in range(NH):
                    pr, hf = h // 2, h % 2
                    kt = 4 * (u + 1)
                    po, otag = PS[2 + h % 2], f"P:{2 + h % 2}"

                    def pv_op(i, h=h, kt=kt, po=po, otag=otag):
                        pair = i // 2
                        pnm, poff = PBS[pair % 3]
                        pcol = poff + 512 * (i % 2)
                        p.op("T", lambda t, i=i, h=h, kt=kt, po=po, pnm=pnm, pcol=pcol: t.matmul(
                            po[0:65, :],
                            A.view("vs", 0, 128, 1040 * i + 65 * h, 65),
                            A.view(pnm, 0, 128, pcol, 512),
                            start=(i == 0), stop=(i == kt - 1)),
                            reads=A.tags("vs", 1040 * i + 65 * h, 65) + A.tags(pnm, pcol, 512),
                            writes=[otag], accum_tags=(otag,))

                    for pair in range(kt // 2):
                        bs = pair % 2
                        for ih in range(2):
                            i = 2 * pair + ih
                            pv, ptag = (PS[ih], f"P:{ih}") if bs == 0 else (PS[4 + ih], f"P:{4 + ih}")
                            p.op("T", lambda t, pr=pr, hf=hf, i=i, pv=pv: t.matmul(
                                pv[:, :],
                                A.view("ks", 64 * hf, 64, 2048 * pr + 128 * i, 128),
                                A.view("qs", 64 * hf, 64, 512 * pr, 512),
                                start=True, stop=True),
                                reads=A.tags("ks", 2048 * pr, 512 * (u + 1)) + A.tags("qs", 512 * pr, 512),
                                writes=[ptag])
                        pset = ps01 if bs == 0 else ps45
                        ptags = ["P:0", "P:1"] if bs == 0 else ["P:4", "P:5"]
                        pnm, poff = PBS[pair % 3]
                        p.op("S", lambda s_, h=h, pnm=pnm, poff=poff, pset=pset: s_.activation(
                            A.view(pnm, 0, 128, poff, 1024), pset[:, :], AF.Exp,
                            bias=negc[:, h:h + 1], scale=0.125),
                            reads=ptags + ["negc"],
                            writes=A.tags(pnm, poff, 1024))
                        for ih in range(2):
                            i = 2 * pair + ih
                            if i >= 4 * u:
                                d = i - 4 * u
                                p.op("V", lambda v, pnm=pnm, poff=poff, ih=ih, d=d: v.tensor_tensor(
                                    A.view(pnm, 0, 128, poff + 512 * ih, 512),
                                    A.view(pnm, 0, 128, poff + 512 * ih, 512),
                                    kmask01[:, 512 - 128 * d:1024 - 128 * d], ALU.mult),
                                    reads=A.tags(pnm, poff + 512 * ih, 512) + ["kmask01"],
                                    writes=A.tags(pnm, poff + 512 * ih, 512))
                        if pair == 0 and pend_tail is not None:
                            tail_b(*pend_tail)
                            pend_tail = None
                        if pair >= 2:
                            pv_op(2 * pair - 4)
                            pv_op(2 * pair - 3)
                    if fill:
                        take = (len(fill) + NH - 1 - h) // (NH - h)
                        for _ in range(take):
                            fill.pop(0)()
                    for i in range(max(kt - 4, 0), kt):
                        pv_op(i)
                    tail_a(po, otag)
                    pend_tail = (pr, hf, po, otag)
                while fill:
                    fill.pop(0)()
                tail_b(*pend_tail)
                for half in range(2):
                    for c in range(8):
                        for t in range(4):
                            pv, ptag = PS[2 + t], f"P:{2 + t}"
                            p.op("T", lambda tt, c=c, t=t, half=half, pv=pv: tt.matmul(
                                pv[:, :],
                                A.view("os", 0, 128, 512 * c + 128 * t, 128),
                                A.view("wos", 0, 128, 1024 * c + 512 * half, 512),
                                start=(c == 0), stop=False),
                                reads=A.tags("os", 512 * c, 512) + A.tags("wos", 1024 * c + 512 * half, 512),
                                writes=[ptag], accum_tags=(ptag,))
                    for t in range(4):
                        pv, ptag = PS[2 + t], f"P:{2 + t}"
                        p.op("T", lambda tt, half=half, pv=pv: tt.matmul(
                            pv[:, :], kones[0:1, 0:128],
                            kbrow(3072 + half * 512, 512),
                            start=False, stop=True),
                            reads=["kones"] + kbt, writes=[ptag], accum_tags=(ptag,))
                        p.op("V", lambda v, t=t, half=half, pv=pv, xo=xo: v.tensor_tensor(
                            Z.view("x2b", 0, 128, t * 512, 512), pv[:, :],
                            Z.view("xa", 0, 128, xo + t * 1024 + 512 * half, 512), ALU.add),
                            reads=[ptag] + Z.tags("xa", xo + t * 1024 + 512 * half, 512),
                            writes=Z.tags("x2b", t * 512, 512))
                        p.op("Y", lambda y, base=base, u=u, t=t, half=half: y.dma_start(
                            out=x2_d[base + 512 * u + 128 * t: base + 512 * u + 128 * (t + 1),
                                     512 * half:512 * (half + 1)],
                            in_=Z.view("x2b", 0, 128, t * 512, 512)),
                            reads=Z.tags("x2b", t * 512, 512), dma=f"dx2{t % 2}")
                if u + 2 <= 3:
                    dxa_issue(u + 2)
        p.fence()

        for c in range(32):
            p.op("G", lambda g, c=c: g.dma_start(
                out=A.view("R2", 0, 128, c * 1024, 1024), in_=wcon_s[c]),
                reads=[f"dram_wc:{c}"],
                writes=A.tags("R2", c * 1024, 1024), dma=f"drc{c % 2}")
        bias2_setup()
        emit_ffn(x2_d, y_out, "f2")
        p.fence()

        p.emit()
    return nc


_NC_CACHE = {}


def _get_nc():
    if "nc" not in _NC_CACHE:
        _NC_CACHE["nc"] = build_nc()
    return _NC_CACHE["nc"]


def _consts():
    ident = np.eye(128, dtype=BFD)
    tident = np.tile(np.eye(128, dtype=np.float32), (1, 32)).astype(BFD)
    pp = np.arange(128)[:, None]
    xx = np.arange(1024)[None, :]
    bigmask = np.where(pp <= xx - 512, 0.0, -30000.0).astype(BFD)
    mask01 = np.where(pp <= xx - 512, 1.0, 0.0).astype(BFD)
    ho = (np.arange(128)[:, None] // 64 == np.arange(128)[None, :] // 64)
    headones = ho.astype(BFD)
    return dict(cident=ident, ctident=tident, cbigmask=bigmask, cmask01=mask01,
                cheadones=headones)


def kernel(**inputs):
    x = np.ascontiguousarray(np.asarray(inputs["L0_attn_x"], np.float32))
    shards = x.reshape(NCORE, TOK, D)
    consts = _consts()
    weights = {nm: np.ascontiguousarray(np.asarray(inputs[nm], np.float32)) for nm in WNAMES}
    nc = _get_nc()
    in_maps = []
    for i in range(NCORE):
        m = {"x": shards[i]}
        m.update(weights)
        m.update(consts)
        in_maps.append(m)
    res = run_bass_kernel_spmd(nc, in_maps, core_ids=list(range(NCORE)))
    out = np.concatenate([np.asarray(r["y"], np.float32).reshape(NSEQ, SEQ, D)
                          for r in res.results], axis=0)
    return out.reshape(32, SEQ, D)
